# version 01
# baseline (speedup 1.0000x reference)
import numpy as np
from contextlib import ExitStack

import concourse.bass as bass
import concourse.mybir as mybir
from concourse.bass_utils import run_bass_kernel_spmd

F32 = mybir.dt.float32
BF16 = mybir.dt.bfloat16
I32 = mybir.dt.int32
AF = mybir.ActivationFunctionType
ALU = mybir.AluOpType
AX = mybir.AxisListType

NCORES = 8
D = 1024
B = 16
N = 2048
NL = N // NCORES
TOK = B * NL
KCD = D // 128

_LAUNCH = {"n": 0}

ENGS = "tvags"


class Prog:

    def __init__(self, name):
        self.nc = bass.Bass("TRN2", target_bir_lowering=False, debug=False)
        self.name = name
        self.stack = ExitStack()
        self.q = {e: [] for e in ENGS}
        self.count = {}
        self.sem_names = []
        self.sems = None
        self.inputs = {}
        self.outputs = {}
        self._nsb = 0

    def inp(self, name, shape, dtype=F32):
        ap = self.nc.dram_tensor(name, list(shape), dtype, kind="ExternalInput").ap()
        self.inputs[name] = ap
        return ap

    def out(self, name, shape, dtype=F32):
        ap = self.nc.dram_tensor(name, list(shape), dtype, kind="ExternalOutput").ap()
        self.outputs[name] = ap
        return ap

    def dram(self, name, shape, dtype=F32):
        return self.nc.dram_tensor(name, list(shape), dtype, kind="Internal").ap()

    def sbuf(self, name, shape, dtype=F32):
        guard = self.nc.sbuf_tensor(name, list(shape), dtype)
        t = guard.__enter__()
        if not hasattr(self, "_guards"):
            self._guards = {}
        self._guards[name] = guard
        return t

    def free_sbuf(self, *names):
        for nm in names:
            self._guards.pop(nm).__exit__(None, None, None)

    def psum(self, name, shape, dtype=F32):
        guard = self.nc.psum_tensor(name, list(shape), dtype)
        t = guard.__enter__()
        if not hasattr(self, "_guards"):
            self._guards = {}
        self._guards[name] = guard
        return t

    def _sem(self, name):
        if name not in self.count:
            self.count[name] = 0
            self.sem_names.append(name)
        return name

    def op(self, eng, builder, sem=None, n=1):
        selfser = eng in "vag"
        if selfser:
            self._sem(eng)
            if self.count[eng] > 0:
                self.wait(eng, eng, self.count[eng])
            self.count[eng] += 1
        tag2 = sem is not None and (sem != eng or not selfser)
        sem = self._sem(sem if sem is not None else eng)
        if not (selfser and sem == eng):
            self.count[sem] += n
        def emit(e, _b=builder, _s=sem, _n=n, _ss=selfser, _eng=eng):
            inst = _b(e)
            if _ss:
                inst.then_inc(self.sems[_eng], 1)
                if _s != _eng:
                    inst.then_inc(self.sems[_s], _n)
            else:
                inst.then_inc(self.sems[_s], _n)
        self.q[eng].append(emit)
        return self.count[sem]

    def op0(self, eng, builder):
        if eng in "vag":
            self.op(eng, builder)
        else:
            self.q[eng].append(lambda e, _b=builder: _b(e))

    def dma(self, eng, out, in_, sem="io", **kw):
        s = self._sem(eng + "." + sem)
        self.count[s] += 16
        def emit(e, _s=s):
            e.dma_start(out=out, in_=in_, **kw).then_inc(self.sems[_s], 16)
        self.q[eng].append(emit)
        return self.count[s]

    def wait_dma(self, eng, issuer, sem, count=None):
        self.wait(eng, issuer + "." + sem, count)

    def wait(self, eng, sem, count=None):
        sem = self._sem(sem)
        if count is None:
            count = self.count[sem]
        if count <= 0:
            return
        self.q[eng].append(lambda e, _s=sem, _c=count: e.wait_ge(self.sems[_s], _c))

    def mark(self, eng):
        return self.op(eng, lambda e: e.nop())

    def fence(self):
        snap = dict(self.count)
        for e in ENGS:
            for nm, c in snap.items():
                if c > 0 and nm != e:
                    self.wait(e, nm, c)

    def finish(self):
        with self.stack:
            self.sems = {nm: self.stack.enter_context(self.nc.semaphore("sem_" + nm))
                         for nm in self.sem_names}
            block = self.stack.enter_context(self.nc.Block())
            qs = self.q

            @block.sync
            def _(h):
                for f in qs["s"]:
                    f(h)
            @block.tensor
            def _(h):
                for f in qs["t"]:
                    f(h)
            @block.vector
            def _(h):
                for f in qs["v"]:
                    f(h)
            @block.scalar
            def _(h):
                for f in qs["a"]:
                    f(h)
            @block.gpsimd
            def _(h):
                for f in qs["g"]:
                    f(h)
                for nm in self.sem_names:
                    if self.count[nm] > 0 and nm not in ENGS:
                        h.wait_ge(self.sems[nm], self.count[nm])
        return self.nc

    def run(self, in_maps, **kw):
        nc = self.finish()
        _LAUNCH["n"] += 1
        res = run_bass_kernel_spmd(nc, in_maps, core_ids=list(range(len(in_maps))), **kw)
        return res.results


def bf16(x):
    import ml_dtypes
    return np.ascontiguousarray(x).astype(ml_dtypes.bfloat16)


def cc(x):
    return np.ascontiguousarray(x)



def build_prep():
    p = Prog("prep")
    w1T = p.inp("w1T", [4096, 1024])
    w2 = p.inp("w2", [4096, 1024])
    b1 = p.inp("b1", [128, 32])
    b2 = p.inp("b2", [1, 1024])
    wg_sh = p.inp("wg_sh", [384, 1024])
    wu_sh = p.inp("wu_sh", [384, 1024])
    wd_sh = p.inp("wd_sh", [128, 3072])
    identB = p.inp("identB", [128, 128], BF16)

    W_out = p.out("W_out", [1024, 1024], BF16)
    c_out = p.out("c_out", [1, 1024])
    wgT_out = p.out("wgT_out", [1024, 384], BF16)
    wuT_out = p.out("wuT_out", [1024, 384], BF16)
    wdT_out = p.out("wdT_out", [3072, 128], BF16)

    s_w1 = p.sbuf("s_w1", [128, 32 * 1024], BF16)
    s_w2 = p.sbuf("s_w2", [128, 32 * 1024], BF16)
    s_b1 = p.sbuf("s_b1", [128, 32])
    s_b1b = p.sbuf("s_b1b", [128, 32], BF16)
    s_b2 = p.sbuf("s_b2", [1, 1024])
    s_W = p.sbuf("s_W", [128, 8 * 1024], BF16)
    s_c = p.sbuf("s_c", [1, 1024])
    s_dq = p.sbuf("s_dq", [128, 3072])
    s_dqb = p.sbuf("s_dqb", [128, 3072], BF16)
    s_mx = p.sbuf("s_mx", [128, 96])
    s_sc = p.sbuf("s_sc", [128, 96])
    s_rc = p.sbuf("s_rc", [128, 96])
    s_int = p.sbuf("s_int", [128, 3072], I32)
    s_r = p.sbuf("s_r", [128, 3072])
    s_id = p.sbuf("s_id", [128, 128], BF16)
    s_tp = p.sbuf("s_tp", [128, 3072], BF16)
    pW = [p.psum("pW0", [128, 512]), p.psum("pW1", [128, 512])]
    pc = p.psum("pc", [1, 1024])
    ptp = p.psum("ptp", [128, 128], BF16)

    p.dma("g", s_w1.rearrange("p (c f) -> p c f", f=1024),
          w1T.rearrange("(c p) f -> p c f", p=128), sem="ld")
    p.dma("g", s_w2.rearrange("p (c f) -> p c f", f=1024),
          w2.rearrange("(c p) f -> p c f", p=128), sem="ld")
    p.dma("s", s_b1[:], b1, sem="ld")
    p.dma("s", s_b2[:], b2, sem="ld")
    p.dma("s", s_id[:], identB, sem="ld")
    p.wait_dma("v", "g", "ld")
    p.wait_dma("v", "s", "ld")
    vm_cast = p.op("v", lambda e: e.tensor_copy(s_b1b[:], s_b1[:]))
    p.wait_dma("t", "g", "ld")
    p.wait_dma("t", "s", "ld")
    p.wait("t", "v", vm_cast)

    evac_marks = {}
    for ic in range(8):
        for jh in range(2):
            bank = (ic * 2 + jh) % 2
            ps = pW[bank]
            key = ic * 2 + jh
            if key - 2 >= 0:
                p.wait("t", "v", evac_marks[key - 2])
            for kc in range(32):
                col = kc * 8 + ic
                bld = lambda e, _k=kc, _c=col, _j=jh, _p=ps: e.matmul(
                    _p[:], s_w1[:, _c * 128:_c * 128 + 128],
                    s_w2[:, _k * 1024 + _j * 512:_k * 1024 + _j * 512 + 512],
                    start=(_k == 0), stop=(_k == 31))
                if kc < 31:
                    p.op0("t", bld)
                else:
                    tmark = p.op("t", bld)
            p.wait("v", "t", tmark)
            evac_marks[key] = p.op("v", lambda e, _i=ic, _j=jh, _p=ps: e.tensor_copy(
                s_W[:, _i * 1024 + _j * 512:_i * 1024 + _j * 512 + 512], _p[:]))
    for jh in range(2):
        for kc in range(32):
            bld = lambda e, _k=kc, _j=jh: e.matmul(
                pc[:, _j * 512:_j * 512 + 512], s_b1b[:, _k:_k + 1],
                s_w2[:, _k * 1024 + _j * 512:_k * 1024 + _j * 512 + 512],
                start=(_k == 0), stop=(_k == 31))
            if jh == 1 and kc == 31:
                tmark = p.op("t", bld)
            else:
                p.op0("t", bld)
    p.wait("v", "t", tmark)
    vm_c = p.op("v", lambda e: e.tensor_tensor(s_c[:], pc[:], s_b2[:], ALU.add))
    p.wait("g", "v", vm_c)
    p.dma("g", W_out.rearrange("(c p) f -> p c f", p=128),
          s_W.rearrange("p (c f) -> p c f", f=1024), sem="st")
    p.dma("g", c_out, s_c[:], sem="st")

    def dequant(src, dst, nchunks, I):
        nblk = I // 32
        for ch in range(nchunks):
            p.wait("s", "v", self_mark[0]) if False else None
            p.dma("s", s_dq[:, 0:I], src[ch * 128:(ch + 1) * 128, :], sem="ldq")
            p.wait_dma("v", "s", "ldq")
            p.op("v", lambda e, _n=nblk: e.reduce_max(
                s_mx.rearrange("p (c o) -> p c o", o=1)[:, 0:_n],
                s_dq.rearrange("p (c i) -> p c i", i=32)[:, 0:_n],
                AX.X, apply_absolute_value=True))
            p.op("v", lambda e, _n=nblk: e.tensor_scalar(
                s_sc[:, 0:_n], s_mx[:, 0:_n], 1.0 / 7.0, 1e-12, ALU.mult, ALU.max))
            p.op("v", lambda e, _n=nblk: e.reciprocal(s_rc[:, 0:_n], s_sc[:, 0:_n]))
            for blk in range(nblk):
                p.op0("v", lambda e, _b=blk: e.tensor_scalar(
                    s_dq[:, _b * 32:_b * 32 + 32], s_dq[:, _b * 32:_b * 32 + 32],
                    s_rc[:, _b:_b + 1], 0.5, ALU.mult, ALU.add))
            p.op("v", lambda e, _I=I: e.tensor_copy(s_int[:, 0:_I], s_dq[:, 0:_I]))
            p.op("v", lambda e, _I=I: e.tensor_copy(s_r[:, 0:_I], s_int[:, 0:_I]))
            p.op("v", lambda e, _I=I: e.tensor_tensor(
                s_dq[:, 0:_I], s_dq[:, 0:_I], s_r[:, 0:_I], ALU.is_lt))
            p.op("v", lambda e, _I=I: e.tensor_tensor(
                s_dq[:, 0:_I], s_r[:, 0:_I], s_dq[:, 0:_I], ALU.subtract))
            for blk in range(nblk):
                bld = lambda e, _b=blk: e.tensor_scalar(
                    s_dqb[:, _b * 32:_b * 32 + 32], s_dq[:, _b * 32:_b * 32 + 32],
                    s_sc[:, _b:_b + 1], None, ALU.mult)
                if blk < nblk - 1:
                    p.op0("v", bld)
                else:
                    vm_dqb = p.op("v", bld)
            p.wait("s", "v", vm_dqb)
            p.wait("t", "v", vm_dqb)
            for icb in range(I // 128):
                tm = p.op("t", lambda e, _i=icb: e.transpose(
                    ptp[:], s_dqb[:, _i * 128:(_i + 1) * 128], s_id[:]))
                p.wait("v", "t", tm)
                if icb == 0:
                    p.wait_dma("v", "g", "sto")
                vm = p.op("v", lambda e, _i=icb: e.tensor_copy(
                    s_tp[:, _i * 128:(_i + 1) * 128], ptp[:]))
                p.wait("t", "v", vm)
            p.wait("g", "v")
            p.dma("g", dst.rearrange("(c p) r -> p c r", p=128)[:, :, ch * 128:(ch + 1) * 128],
                  s_tp.rearrange("p (c r) -> p c r", r=128)[:, 0:I // 128, :], sem="sto")

    dequant(wg_sh, wgT_out, 3, 1024)
    dequant(wu_sh, wuT_out, 3, 1024)
    dequant(wd_sh, wdT_out, 1, 3072)
    return p


def run_prep(inputs, consts):
    p = build_prep()
    in_maps = []
    for c in range(NCORES):
        in_maps.append(dict(
            w1T=cc(inputs["L0_ffn_w1"][c].T),
            w2=cc(inputs["L0_ffn_w2"][c]),
            b1=cc(inputs["L0_ffn_b1"][c].reshape(32, 128).T),
            b2=cc(inputs["L0_ffn_b2"][c].reshape(1, 1024)),
            wg_sh=cc(inputs["L1_ffn_w_gate"][c * 384:(c + 1) * 384]),
            wu_sh=cc(inputs["L1_ffn_w_up"][c * 384:(c + 1) * 384]),
            wd_sh=cc(inputs["L1_ffn_w_down"][c * 128:(c + 1) * 128]),
            identB=consts["identB"],
        ))
    res = p.run(in_maps)
    W_all = np.stack([np.asarray(r["W_out"]) for r in res])
    C_all = np.stack([np.asarray(r["c_out"])[0] for r in res])
    wgT = np.concatenate([np.asarray(r["wgT_out"]) for r in res], axis=1)
    wuT = np.concatenate([np.asarray(r["wuT_out"]) for r in res], axis=1)
    wdT = np.concatenate([np.asarray(r["wdT_out"]) for r in res], axis=1)
    return W_all, C_all, wgT, wuT, wdT


def make_consts():
    import ml_dtypes
    c = {}
    c["identB"] = np.eye(128, dtype=ml_dtypes.bfloat16)
    c["identF"] = np.eye(128, dtype=np.float32)
    m = np.zeros((128, 128), dtype=np.float32)
    for i in range(8):
        m[i * 16:(i + 1) * 16, i * 16:(i + 1) * 16] = 1.0
    c["mha_mask"] = np.tile(m, (1, 4)).astype(ml_dtypes.bfloat16)
    tri = np.tril(np.ones((128, 128), dtype=np.float32)).T
    c["gla_mask"] = tri.astype(ml_dtypes.bfloat16)
    li = np.zeros((16, TOK), dtype=np.float32)
    for t in range(TOK):
        li[t % 16, t] = 1.0
    c["lind"] = li
    c["iota8"] = np.tile(np.arange(8, dtype=np.float32), (128, 1))
    return c


def kernel(**inputs):
    _LAUNCH["n"] = 0
    consts = make_consts()
    W_all, C_all, wgT, wuT, wdT = run_prep(inputs, consts)

    x = np.asarray(inputs["L0_attn_x"])

    pA_ = build_l0_front()
    ims = []
    for c in range(NCORES):
        xs = cc(x[:, c * NL:(c + 1) * NL, :].transpose(1, 0, 2).reshape(TOK, D))
        ims.append(dict(
            x=xs,
            w_qkv=cc(inputs["L0_attn_w_qkv"]),
            bq_row=cc(inputs["L0_attn_b_qkv"].reshape(1, 3072)),
            bq_col=cc(inputs["L0_attn_b_qkv"].reshape(24, 128).T),
            ipjT=cc(inputs["L0_ffn_in_proj_w"].T),
            ipb_row=cc(inputs["L0_ffn_in_proj_b"].reshape(1, 3072)),
            ipb_col=cc(inputs["L0_ffn_in_proj_b"].reshape(24, 128).T),
            opjT=cc(inputs["L0_ffn_out_proj_w"].T),
            opb_row=cc(inputs["L0_ffn_out_proj_b"].reshape(1, 1024)),
            identB=consts["identB"], identF=consts["identF"],
            mham=consts["mha_mask"],
            mhamF=cc(np.asarray(consts["mha_mask"], np.float32)),
            bip_col=cc(inputs["L0_ffn_in_proj_b"][0:2048].reshape(16, 128).T),
        ))
    resA = pA_.run(ims)
    kvp_all = np.stack([np.asarray(r["kvp_o"]) for r in resA])

    pB_ = build_l0_back()
    ims = []
    for c in range(NCORES):
        ims.append(dict(
            xn=np.asarray(resA[c]["xn_o"]), mha=np.asarray(resA[c]["mha_o"]),
            qe=np.asarray(resA[c]["qe_o"]), kvp=kvp_all,
            W_all=W_all, C_all=C_all,
            wg=cc(inputs["L0_ffn_wg"]),
            ln1g_row=cc(inputs["L0_ffn_ln1_g"].reshape(1, 1024)),
            ln1b_row=cc(inputs["L0_ffn_ln1_b"].reshape(1, 1024)),
            w_proj=cc(inputs["L0_attn_w_proj"]),
            bp_row=cc(inputs["L0_attn_b_proj"].reshape(1, 1024)),
            identB=consts["identB"], identF=consts["identF"],
            iotam9=cc(consts["iota8"] - 9.0),
        ))
    resB = pB_.run(ims)
    cnt_all = np.stack([np.asarray(r["cnt_o"]).reshape(128) for r in resB])

    pC_ = build_l0_tail()
    ims = []
    for c in range(NCORES):
        mb = np.zeros((8, 1), np.float32)
        mb[:c] = 1.0
        ims.append(dict(
            x=ims_x_cache[c] if False else cc(x[:, c * NL:(c + 1) * NL, :].transpose(1, 0, 2).reshape(TOK, D)),
            y1=np.asarray(resB[c]["y1_o"]), moe=np.asarray(resB[c]["moe_o"]),
            s_in=np.asarray(resB[c]["s_o"]),
            cnt_all=cnt_all, pref=np.asarray(resB[c]["pref_o"]),
            ohF=np.asarray(resB[c]["oh_o"]),
            ln2g_row=cc(inputs["L0_ffn_ln2_g"].reshape(1, 1024)),
            ln2b_row=cc(inputs["L0_ffn_ln2_b"].reshape(1, 1024)),
            mask_below=mb, lind=consts["lind"],
        ))
    resC = pC_.run(ims)
    x1_full = np.zeros((B, N, D), np.float32)
    for c in range(NCORES):
        x1c = np.asarray(resC[c]["x1_o"]).reshape(NL, B, D)
        x1_full[:, c * NL:(c + 1) * NL, :] = x1c.transpose(1, 0, 2)

    pD_ = build_l1_attn()
    ims = []
    for c in range(NCORES):
        ims.append(dict(
            x1=cc(x1_full[2 * c:2 * c + 2].reshape(TOK, D)),
            WqT=cc(inputs["L1_attn_Wq"].T), WkT=cc(inputs["L1_attn_Wk"].T),
            WvT=cc(inputs["L1_attn_Wv"].T),
            identB=consts["identB"], glam=consts["gla_mask"],
        ))
    resD = pD_.run(ims)
    max1_row = cc(np.concatenate([np.asarray(r["max1_o"]).reshape(1, 1) for r in resD], axis=1))

    pF1 = build_l1_up()
    ims = []
    for c in range(NCORES):
        ims.append(dict(
            xn1=np.asarray(resD[c]["xn1_o"]), max1=max1_row,
            wuT=wuT, wgT=wgT, identB=consts["identB"],
        ))
    resF1 = pF1.run(ims)
    maxug_row = cc(np.concatenate([np.asarray(r["maxug_o"]).reshape(1, 2) for r in resF1], axis=1))

    pF2 = build_l1_mid()
    ims = [dict(up=np.asarray(resF1[c]["up_o"]), gate=np.asarray(resF1[c]["gate_o"]),
                maxug=maxug_row, identB=consts["identB"]) for c in range(NCORES)]
    resF2 = pF2.run(ims)
    maxp_row = cc(np.concatenate([np.asarray(r["maxp_o"]).reshape(1, 1) for r in resF2], axis=1))

    pF3 = build_l1_mid2()
    ims = [dict(prod=np.asarray(resF2[c]["prod_o"]), upq=np.asarray(resF2[c]["upq_o"]),
                maxp=maxp_row, identB=consts["identB"]) for c in range(NCORES)]
    resF3 = pF3.run(ims)
    maxp2_row = cc(np.concatenate([np.asarray(r["maxp2_o"]).reshape(1, 1) for r in resF3], axis=1))

    pG_ = build_l1_tail()
    ims = []
    for c in range(NCORES):
        ims.append(dict(
            prod2=np.asarray(resF3[c]["prod2_o"]), maxp2=maxp2_row,
            o_gla=np.asarray(resD[c]["o_o"]),
            x1=cc(x1_full[2 * c:2 * c + 2].reshape(TOK, D)),
            wdT=wdT, WoT=cc(inputs["L1_attn_Wo"].T),
            normw_col=cc(inputs["L1_attn_norm_w"].reshape(8, 128).T),
            identB=consts["identB"],
        ))
    resG = pG_.run(ims)
    out = np.concatenate([np.asarray(r["out_o"]).reshape(2, N, D) for r in resG], axis=0)
    return out


if __name__ == "__main__":
    pass



def build_l0_front():
    p = Prog("l0_front")
    x = p.inp("x", [TOK, D])
    w_qkv = p.inp("w_qkv", [1024, 3072])
    bq_row = p.inp("bq_row", [1, 3072])
    bq_col = p.inp("bq_col", [128, 24])
    ipjT = p.inp("ipjT", [1024, 3072])
    ipb_row = p.inp("ipb_row", [1, 3072])
    ipb_col = p.inp("ipb_col", [128, 24])
    opjT = p.inp("opjT", [1024, 1024])
    opb_row = p.inp("opb_row", [1, 1024])
    identB = p.inp("identB", [128, 128], BF16)
    identF = p.inp("identF", [128, 128])
    mham = p.inp("mham", [128, 512], BF16)
    mhamF = p.inp("mhamF", [128, 512])
    bip_col = p.inp("bip_col", [128, 16])

    xn_o = p.out("xn_o", [TOK, D])
    qe_o = p.out("qe_o", [1024, TOK], BF16)
    kvp_o = p.out("kvp_o", [128, 8320])
    mha_o = p.out("mha_o", [TOK, D])

    s_stat = p.sbuf("s_stat", [128, 4 * 32])
    s_xn = p.sbuf("s_xn", [128, 1024])
    s_xnb = p.sbuf("s_xnb", [128, 1024], BF16)
    s_id = p.sbuf("s_id", [128, 128], BF16)
    s_bqc = p.sbuf("s_bqc", [128, 24])
    s_bipc = p.sbuf("s_bipc", [128, 16])
    s_rowsF2 = p.sbuf("s_rowsF2", [1, 3072])
    s_rowsF3 = p.sbuf("s_rowsF3", [1, 1024])
    s_ones = p.sbuf("s_ones", [1, 128], BF16)
    s_onecol = p.sbuf("s_onecol", [128, 1], BF16)
    s_kbias = p.sbuf("s_kbias", [128, 1024])
    s_vbias = p.sbuf("s_vbias", [128, 1024])
    s_rowsF = p.sbuf("s_rowsF", [1, 3072])
    s_rows2 = p.sbuf("s_rows2", [1, 1024], BF16)
    s_w = p.sbuf("s_w", [128, 8 * 3072], BF16)
    s_xnF = p.sbuf("s_xnF", [128, KCD * TOK], BF16)
    s_qe = p.sbuf("s_qe", [128, 2 * 4096], BF16)

    s_x = p.sbuf("s_x", [128, 2 * 1024])
    s_t1 = p.sbuf("s_t1", [128, 1024])
    s_t2 = p.sbuf("s_t2", [128, 1024])
    s_kv = p.sbuf("s_kv", [128, 520])
    s_ktm = p.sbuf("s_ktm", [128, 2 * 1024], BF16)
    s_vtm = p.sbuf("s_vtm", [128, 2 * 1024], BF16)
    pA = p.psum("pA", [128, 512])
    pB = p.psum("pB", [128, 512])
    pC = p.psum("pC", [128, 512])
    pD = p.psum("pD", [128, 512])
    pT = p.psum("pT", [128, 1024], BF16)
    pKV0 = p.psum("pKV0", [64, 65])
    pKV1 = p.psum("pKV1", [64, 65])

    p.dma("s", s_id[:], identB, sem="ld")
    p.dma("s", s_bqc[:], bq_col, sem="ld")
    p.dma("s", s_rowsF[:], bq_row, sem="ld")
    p.dma("s", s_bipc[:], bip_col, sem="ld")
    p.dma("s", s_rowsF2[:], ipb_row, sem="ld")
    p.dma("s", s_rowsF3[:], opb_row, sem="ld")
    p.dma("g", s_w.rearrange("p (c f) -> p c f", f=3072),
          w_qkv.rearrange("(c p) f -> p c f", p=128), sem="w0")
    p.op("g", lambda e: e.memset(s_ones[:], 1.0))
    p.op("g", lambda e: e.memset(s_onecol[:], 1.0))
    kv_evac = {}

    NT = TOK // 128
    inv_d = 1.0 / D
    for tc in range(NT):
        buf = tc % 2
        xs = s_x[:, buf * 1024:(buf + 1) * 1024]
        if tc >= 2:
            p.wait("s", "v", p.count["v"])
        ldm = p.dma("s", xs, x[tc * 128:(tc + 1) * 128, :], sem="ldx%d" % buf)
        p.wait("v", "s.ldx%d" % buf, ldm)
        p.op("v", lambda e, _x=xs, _t=tc: e.reduce_sum(
            s_stat.rearrange("p (c o) -> p c o", o=1)[:, _t:_t + 1], _x, AX.X))
        p.wait("a", "s.ldx%d" % buf, ldm)
        p.wait("a", "v")
        p.op("a", lambda e, _x=xs: e.activation(s_t1[:, 0:1024], _x, AF.Square))
        p.wait("v", "a")
        p.op("v", lambda e, _t=tc: e.reduce_sum(
            s_stat.rearrange("p (c o) -> p c o", o=1)[:, 32 + _t:33 + _t], s_t1[:, 0:1024], AX.X))
    p.op("v", lambda e: e.tensor_scalar(s_stat[:, 64:96], s_stat[:, 0:32], inv_d, None, ALU.mult))
    p.op("v", lambda e: e.tensor_tensor(s_t1[:, 0:32], s_stat[:, 64:96], s_stat[:, 64:96], ALU.mult))
    p.op("v", lambda e: e.tensor_scalar(s_stat[:, 96:128], s_stat[:, 32:64], inv_d, None, ALU.mult))
    p.op("v", lambda e: e.tensor_tensor(s_stat[:, 96:128], s_stat[:, 96:128], s_t1[:, 0:32], ALU.subtract))
    p.op("v", lambda e: e.tensor_scalar(s_stat[:, 96:128], s_stat[:, 96:128], 1.0, 1e-5, ALU.mult, ALU.add))
    p.wait("a", "v")
    vmark = p.op("a", lambda e: e.activation(s_t1[:, 32:64], s_stat[:, 96:128], AF.Sqrt))
    p.wait("v", "a", vmark)
    p.op("v", lambda e: e.reciprocal(s_stat[:, 96:128], s_t1[:, 32:64]))

    p.wait("t", "s.ld")
    for tc in range(NT):
        buf = tc % 2
        xs = s_x[:, buf * 1024:(buf + 1) * 1024]
        xns = s_xn[:, 0:1024]
        gate = p.count["v"]
        p.wait("s", "v", gate)
        if tc > 0:
            p.wait("s", "g.sxn", tc * 16)
        ldm = p.dma("s", xs, x[tc * 128:(tc + 1) * 128, :], sem="ldx%d" % buf)
        p.wait("v", "s.ldx%d" % buf, ldm)
        p.op("v", lambda e, _x=xs, _o=xns, _t=tc: e.tensor_scalar(
            _o, _x, s_stat[:, 64 + _t:65 + _t], s_stat[:, 96 + _t:97 + _t],
            ALU.subtract, ALU.mult))
        vm = p.op("v", lambda e, _o=xns: e.tensor_copy(s_xnb[:], _o))
        p.wait("g", "v", vm)
        p.dma("g", xn_o[tc * 128:(tc + 1) * 128, :], xns, sem="sxn")
        p.wait("t", "v", vm)
        for kc in range(KCD):
            last = (kc == KCD - 1)
            bld = (lambda e, _k=kc: e.transpose(
                pT[:, _k * 128:(_k + 1) * 128], s_xnb[:, _k * 128:(_k + 1) * 128], s_id[:]))
            tm = p.op("t", bld) if last else p.op0("t", bld)
        p.wait("v", "t", tm)
        vm2 = p.op("v", lambda e, _t=tc: e.tensor_copy(
            s_xnF.rearrange("p (c f) -> p c f", f=TOK)[:, :, _t * 128:(_t + 1) * 128],
            pT.rearrange("p (c f) -> p c f", f=128)))
        p.wait("t", "v", vm2)
    xnF = lambda kc: s_xnF[:, kc * TOK:(kc + 1) * TOK]

    p.wait("t", "g.w0")
    p.wait("a", "s.ld")
    p.op("a", lambda e: e.activation(s_rows2[:], s_rowsF[:, 1024:2048], AF.Copy))
    am = p.op("a", lambda e: e.nop())
    p.wait("t", "a", am)
    p.wait("t", "g")
    kb_t = p.op("t", lambda e: e.matmul(pA[:], s_ones[:], s_rows2[:, 0:512], start=True, stop=True))
    kb_t2 = p.op("t", lambda e: e.matmul(pB[:], s_ones[:], s_rows2[:, 512:1024], start=True, stop=True))
    p.wait("v", "t", kb_t2)
    p.op("v", lambda e: e.tensor_copy(s_kbias[:, 0:512], pA[:]))
    vm = p.op("v", lambda e: e.tensor_copy(s_kbias[:, 512:1024], pB[:]))
    p.wait("a", "v", vm)
    p.op("a", lambda e: e.activation(s_rows2[:], s_rowsF[:, 2048:3072], AF.Copy))
    am = p.op("a", lambda e: e.nop())
    p.wait("t", "a", am)
    vb_t = p.op("t", lambda e: e.matmul(pA[:], s_ones[:], s_rows2[:, 0:512], start=True, stop=True))
    vb_t2 = p.op("t", lambda e: e.matmul(pB[:], s_ones[:], s_rows2[:, 512:1024], start=True, stop=True))
    p.wait("v", "t", vb_t2)
    p.op("v", lambda e: e.tensor_copy(s_vbias[:, 0:512], pA[:]))
    p.op("v", lambda e: e.tensor_copy(s_vbias[:, 512:1024], pB[:]))

    for oc in range(8):
        qbuf = oc % 2
        qch = s_qe[:, qbuf * 4096:(qbuf + 1) * 4096]
        if oc >= 2:
            p.wait("t", "g.sqe%d" % qbuf, (oc // 2) * 16)
        for tgg in range(2):
            banks = [pA, pB, pC, pD]
            p.wait("t", "v")
            for tgi in range(4):
                for kc in range(8):
                    tg = tgg * 4 + tgi
                    last = (kc == 7 and tgi == 3)
                    bld = (lambda e, _k=kc, _o=oc, _t=tg, _b=banks[tgi]: e.matmul(
                        _b[:], s_w[:, _k * 3072 + _o * 128:_k * 3072 + _o * 128 + 128],
                        xnF(_k)[:, _t * 512:(_t + 1) * 512],
                        start=(_k == 0), stop=(_k == 7)))
                    tm = p.op("t", bld) if last else p.op0("t", bld)
            p.wait("v", "t", tm)
            for tgi in range(4):
                tg = tgg * 4 + tgi
                qsl = slice(tg * 512, (tg + 1) * 512)
                sl = slice((tgi % 2) * 512, (tgi % 2) * 512 + 512)
                p.op("v", lambda e, _b=banks[tgi], _s=sl, _o=oc: e.tensor_scalar(
                    s_t1[:, _s], _b[:], s_bqc[:, _o:_o + 1], None, ALU.add))
                p.op("v", lambda e, _s=sl: e.tensor_scalar(
                    s_t2[:, _s], s_t1[:, _s], 0.0, None, ALU.min))
                vm = p.count["v"]
                p.wait("a", "v", vm)
                am = p.op("a", lambda e, _s=sl: e.activation(s_t2[:, _s], s_t2[:, _s], AF.Exp))
                p.wait("v", "a", am)
                p.op("v", lambda e, _s=sl, _qs=qsl, _q=qch: e.scalar_tensor_tensor(
                    _q[:, _qs], s_t1[:, _s], 0.0, s_t2[:, _s], ALU.max, ALU.add))
        vm = p.op("v", lambda e: e.nop())
        p.wait("g", "v", vm)
        p.dma("g", qe_o[oc * 128:(oc + 1) * 128, :], qch, sem="sqe%d" % qbuf)

    p.wait("t", "v")
    for b in range(B):
        for half in range(2):
            for ohalf in range(2):
                ps = pA if ohalf == 0 else pB
                for kc in range(8):
                    last = (kc == 7)
                    bld = (lambda e, _k=kc, _b=b, _h=half, _oh=ohalf, _p=ps: e.matmul(
                        _p[:],
                        xnF(_k).rearrange("p (n l) -> p n l", l=16)[:, _h * 128:(_h + 1) * 128, _b],
                        s_w[:, _k * 3072 + 1024 + _oh * 512:_k * 3072 + 1024 + (_oh + 1) * 512],
                        start=(_k == 0), stop=(_k == 7)))
                    tmk = p.op("t", bld) if last else p.op0("t", bld)
            p.wait("v", "t", tmk)
            ksl = slice(half * 1024, (half + 1) * 1024)
            p.op("v", lambda e: e.tensor_tensor(s_t1[:, 0:512], pA[:], s_kbias[:, 0:512], ALU.add))
            p.op("v", lambda e: e.tensor_tensor(s_t1[:, 512:1024], pB[:], s_kbias[:, 512:1024], ALU.add))
            vm_kread = p.op("v", lambda e: e.tensor_scalar(s_t2[:, 0:1024], s_t1[:, 0:1024], 0.0, None, ALU.min))
            p.wait("a", "v", vm_kread)
            am = p.op("a", lambda e: e.activation(s_t2[:, 0:1024], s_t2[:, 0:1024], AF.Exp))
            p.wait("v", "a", am)
            p.op("v", lambda e, _s=ksl: e.scalar_tensor_tensor(
                s_ktm[:, _s], s_t1[:, 0:1024], 0.0, s_t2[:, 0:1024], ALU.max, ALU.add))
            p.wait("t", "v", vm_kread)
            for ohalf in range(2):
                ps = pC if ohalf == 0 else pD
                for kc in range(8):
                    last = (kc == 7)
                    bld = (lambda e, _k=kc, _b=b, _h=half, _oh=ohalf, _p=ps: e.matmul(
                        _p[:],
                        xnF(_k).rearrange("p (n l) -> p n l", l=16)[:, _h * 128:(_h + 1) * 128, _b],
                        s_w[:, _k * 3072 + 2048 + _oh * 512:_k * 3072 + 2048 + (_oh + 1) * 512],
                        start=(_k == 0), stop=(_k == 7)))
                    tmv = p.op("t", bld) if last else p.op0("t", bld)
            p.wait("v", "t", tmv)
            p.op("v", lambda e, _s=slice(half * 1024, half * 1024 + 512): e.tensor_tensor(
                s_vtm[:, _s], pC[:], s_vbias[:, 0:512], ALU.add))
            vm_v = p.op("v", lambda e, _s=slice(half * 1024 + 512, half * 1024 + 1024): e.tensor_tensor(
                s_vtm[:, _s], pD[:], s_vbias[:, 512:1024], ALU.add))
            p.wait("t", "v", vm_v)
        for h in range(16):
            pair = b * 16 + h
            pkv = pKV0 if h % 2 == 0 else pKV1
            if pair >= 2:
                p.wait("t", "v", kv_evac[pair - 2])
            for hf in range(2):
                p.op0("t", lambda e, _h=h, _hf=hf, _p=pkv: e.matmul(
                    _p[:, 0:64],
                    s_ktm[:, _hf * 1024 + _h * 64:_hf * 1024 + _h * 64 + 64],
                    s_vtm[:, _hf * 1024 + _h * 64:_hf * 1024 + _h * 64 + 64],
                    start=(_hf == 0), stop=(_hf == 1)))
            for hf in range(2):
                bld = (lambda e, _h=h, _hf=hf, _p=pkv: e.matmul(
                    _p[:, 64:65],
                    s_ktm[:, _hf * 1024 + _h * 64:_hf * 1024 + _h * 64 + 64],
                    s_onecol[:, 0:1],
                    start=(_hf == 0), stop=(_hf == 1)))
                tm = p.op("t", bld) if hf == 1 else p.op0("t", bld)
            p.wait("v", "t", tm)
            kv_evac[pair] = p.op("v", lambda e, _h=h, _p=pkv: e.tensor_copy(
                s_kv[(_h % 2) * 64:(_h % 2) * 64 + 64,
                     (_h // 2) * 65:(_h // 2) * 65 + 65],
                _p[:, 0:65]))
        vm = p.count["v"]
        p.wait("g", "v", vm)
        p.dma("g", kvp_o[:, b * 520:(b + 1) * 520], s_kv[:], sem="skv")
        if b < B - 1:
            p.wait("v", "g.skv", (b + 1) * 16)

    p.fence()
    p.free_sbuf("pKV1", "pKV0", "pT", "s_vtm", "s_ktm", "s_kv", "s_t2", "s_t1", "s_x",
                "s_qe", "s_xnF", "s_w", "s_rows2", "s_rowsF")
    s_idF = p.sbuf("s_idF", [128, 128])
    s_wqk = p.sbuf("s_wqk", [128, 8 * 2048])
    s_wv = p.sbuf("s_wv", [128, 8 * 1024])
    s_xnS = p.sbuf("s_xnS", [128, 8 * 256])
    s_xt2 = p.sbuf("s_xt2", [128, 1024])
    s_qS = p.sbuf("s_qS", [128, 8 * 256])
    s_kS = p.sbuf("s_kS", [128, 8 * 256])
    s_vT = p.sbuf("s_vT", [128, 2 * 1024])
    s_vbF = p.sbuf("s_vbF", [128, 1024])
    s_onesF = p.sbuf("s_onesF", [1, 128])
    s_att = p.sbuf("s_att", [128, 512])
    s_attT = p.sbuf("s_attT", [128, 512])
    s_rm = p.sbuf("s_rm", [128, 12])
    s_ost = p.sbuf("s_ost", [128, 1024])
    s_mhamF = p.sbuf("s_mhamF", [128, 512])
    pS = p.psum("pS", [128, 512])
    pTf = p.psum("pTf", [128, 512])
    o_spill = p.dram("o_spill", [1024, TOK], F32)
    p.dma("s", s_idF[:], identF, sem="ld2")
    p.dma("s", s_mhamF[:], mhamF, sem="ld2")
    p.dma("g", s_wqk.rearrange("p (c f) -> p c f", f=2048),
          ipjT.rearrange("(c p) f -> p c f", p=128)[:, :, 0:2048], sem="w1")
    p.dma("g", s_wv.rearrange("p (c f) -> p c f", f=1024),
          ipjT.rearrange("(c p) f -> p c f", p=128)[:, :, 2048:3072], sem="w1")
    p.op("g", lambda e: e.memset(s_onesF[:], 1.0))
    p.wait("t", "s.ld")
    p.wait("t", "s.ld2")
    p.wait("t", "g")
    b1t = p.op("t", lambda e: e.matmul(pA[:], s_onesF[:], s_rowsF2[0:1, 2048:2560], start=True, stop=True))
    b2t = p.op("t", lambda e: e.matmul(pB[:], s_onesF[:], s_rowsF2[0:1, 2560:3072], start=True, stop=True))
    p.wait("v", "t", b2t)
    p.op("v", lambda e: e.tensor_copy(s_vbF[:, 0:512], pA[:]))
    vmb = p.op("v", lambda e: e.tensor_copy(s_vbF[:, 512:1024], pB[:]))
    p.wait("t", "g.w1")
    p.wait("t", "v", vmb)

    for sg in range(16):
        for tci in range(2):
            tc = sg * 2 + tci
            gate = p.count["v"]
            p.wait("s", "v", gate)
            lm = p.dma("s", s_xt2[:], xn_o[tc * 128:(tc + 1) * 128, :], sem="lx2")
            p.wait("t", "s.lx2", lm)
            p.wait("t", "v")
            for kc in range(8):
                kb = kc % 4
                if kc in (4,):
                    p.wait("t", "v")
                bld = (lambda e, _k=kc, _b=kb: e.transpose(
                    pTf[:, _b * 128:(_b + 1) * 128], s_xt2[:, _k * 128:(_k + 1) * 128], s_idF[:]))
                tm = p.op("t", bld)
                if kc == 3 or kc == 7:
                    p.wait("v", "t", tm)
                    base = 0 if kc == 3 else 4
                    p.op("v", lambda e, _t=tci, _ba=base: e.tensor_copy(
                        s_xnS.rearrange("p (c f) -> p c f", f=256)[:, _ba:_ba + 4, _t * 128:(_t + 1) * 128],
                        pTf.rearrange("p (c f) -> p c f", f=128)))
        vxn = p.count["v"]
        p.wait("t", "v", vxn)
        for part in range(2):
            dst = s_qS if part == 0 else s_kS
            for oc in range(8):
                ps = pA if oc % 2 == 0 else pB
                p.wait("t", "v")
                for kc in range(8):
                    last = (kc == 7)
                    bld = (lambda e, _k=kc, _o=oc, _pp=part, _p=ps: e.matmul(
                        _p[:, 0:256], s_wqk[:, _k * 2048 + _pp * 1024 + _o * 128:_k * 2048 + _pp * 1024 + _o * 128 + 128],
                        s_xnS[:, _k * 256:(_k + 1) * 256],
                        start=(_k == 0), stop=(_k == 7)))
                    tm = p.op("t", bld) if last else p.op0("t", bld)
                p.wait("v", "t", tm)
                p.op("v", lambda e, _o=oc, _pp=part, _p=ps, _d=dst: e.tensor_scalar(
                    _d[:, _o * 256:(_o + 1) * 256], _p[:, 0:256],
                    s_bipc[:, _pp * 8 + _o:_pp * 8 + _o + 1], None, ALU.add))
        for tci in range(2):
            for oh in range(2):
                ps = pC if oh == 0 else pD
                p.wait("t", "v")
                for kc in range(8):
                    last = (kc == 7)
                    bld = (lambda e, _k=kc, _t=tci, _oh=oh, _p=ps: e.matmul(
                        _p[:], s_xnS[:, _k * 256 + _t * 128:_k * 256 + _t * 128 + 128],
                        s_wv[:, _k * 1024 + _oh * 512:_k * 1024 + (_oh + 1) * 512],
                        start=(_k == 0), stop=(_k == 7)))
                    tm = p.op("t", bld) if last else p.op0("t", bld)
                p.wait("v", "t", tm)
                p.op("v", lambda e, _t=tci, _oh=oh, _p=ps: e.tensor_tensor(
                    s_vT[:, _t * 1024 + _oh * 512:_t * 1024 + (_oh + 1) * 512],
                    _p[:], s_vbF[:, _oh * 512:(_oh + 1) * 512], ALU.add))
        for tci in range(2):
            tc = sg * 2 + tci
            p.wait("t", "v")
            for h in range(4):
                for hi in range(2):
                    last = (h == 3 and hi == 1)
                    bld = (lambda e, _h=h, _hi=hi, _i=tci: e.matmul(
                        pS[:, _h * 128:(_h + 1) * 128],
                        s_qS[:, (2 * _h + _hi) * 256 + _i * 128:(2 * _h + _hi) * 256 + (_i + 1) * 128],
                        s_kS[:, (2 * _h + _hi) * 256 + _i * 128:(2 * _h + _hi) * 256 + (_i + 1) * 128],
                        start=(_hi == 0), stop=(_hi == 1)))
                    tm = p.op("t", bld) if last else p.op0("t", bld)
            p.wait("v", "t", tm)
            p.op("v", lambda e: e.reduce_max(
                s_rm.rearrange("p (c o) -> p c o", o=1)[:, 0:4],
                pS.rearrange("p (c f) -> p c f", f=128), AX.X))
            vmrm = p.op("v", lambda e: e.tensor_scalar(
                s_rm[:, 4:8], s_rm[:, 0:4], -1.0 / 16.0, None, ALU.mult))
            p.wait("a", "v", vmrm)
            for h in range(4):
                am = p.op("a", lambda e, _h=h: e.activation(
                    s_att[:, _h * 128:(_h + 1) * 128], pS[:, _h * 128:(_h + 1) * 128],
                    AF.Exp, bias=s_rm[:, 4 + _h:5 + _h], scale=1.0 / 16.0))
            p.wait("v", "a", am)
            p.op("v", lambda e: e.tensor_tensor(s_att[:], s_att[:], s_mhamF[:], ALU.mult))
            p.op("v", lambda e: e.reduce_sum(
                s_rm.rearrange("p (c o) -> p c o", o=1)[:, 8:12],
                s_att.rearrange("p (c f) -> p c f", f=128), AX.X))
            p.op("v", lambda e: e.reciprocal(s_rm[:, 8:12], s_rm[:, 8:12]))
            vmsc = None
            for h in range(4):
                vmsc = p.op("v", lambda e, _h=h: e.tensor_scalar(
                    s_att[:, _h * 128:(_h + 1) * 128], s_att[:, _h * 128:(_h + 1) * 128],
                    s_rm[:, 8 + _h:9 + _h], None, ALU.mult))
            p.wait("t", "v", vmsc)
            for h in range(4):
                bld = (lambda e, _h=h: e.transpose(
                    pTf[:, _h * 128:(_h + 1) * 128], s_att[:, _h * 128:(_h + 1) * 128], s_idF[:]))
                tm = p.op("t", bld) if h == 3 else p.op0("t", bld)
            p.wait("v", "t", tm)
            vmat = p.op("v", lambda e: e.tensor_copy(s_attT[:], pTf[:]))
            p.wait("t", "v", vmat)
            for h in range(4):
                for half in range(2):
                    ch = 2 * h + half
                    ps = pC if ch < 4 else pD
                    last = (h == 3 and half == 1)
                    bld = (lambda e, _h=h, _hf=half, _tci=tci, _ps=ps, _ch=ch: e.matmul(
                        _ps[:, (_ch % 4) * 128:(_ch % 4) * 128 + 128],
                        s_vT[:, _tci * 1024 + _h * 256 + _hf * 128:_tci * 1024 + _h * 256 + (_hf + 1) * 128],
                        s_attT[:, _h * 128:(_h + 1) * 128],
                        start=True, stop=True))
                    tm = p.op("t", bld) if last else p.op0("t", bld)
            p.wait("v", "t", tm)
            p.op("v", lambda e: e.tensor_copy(s_ost[:, 0:512], pC[:]))
            vmo = p.op("v", lambda e: e.tensor_copy(s_ost[:, 512:1024], pD[:]))
            p.wait("g", "v", vmo)
            p.dma("g", o_spill.rearrange("(c p) t -> p c t", p=128)[:, :, tc * 128:(tc + 1) * 128],
                  s_ost.rearrange("p (c f) -> p c f", f=128), sem="so%d" % (tc % 2))
            if tci < 1 or sg < 15:
                p.wait("v", "g.so%d" % (tc % 2))

    tmark = p.count["t"]
    p.wait("g", "t", tmark)
    p.dma("g", s_wqk.rearrange("p (c f) -> p c f", f=2048)[:, :, 0:1024],
          opjT.rearrange("(c p) f -> p c f", p=128), sem="w2")
    p.wait("t", "v")
    b1t = p.op("t", lambda e: e.matmul(pA[:], s_onesF[:], s_rowsF3[0:1, 0:512], start=True, stop=True))
    b2t = p.op("t", lambda e: e.matmul(pB[:], s_onesF[:], s_rowsF3[0:1, 512:1024], start=True, stop=True))
    p.wait("v", "t", b2t)
    p.op("v", lambda e: e.tensor_copy(s_vbF[:, 0:512], pA[:]))
    vmb = p.op("v", lambda e: e.tensor_copy(s_vbF[:, 512:1024], pB[:]))
    p.wait("t", "g.w2")
    p.wait("t", "v", vmb)
    p.wait("s", "g.so0")
    p.wait("s", "g.so1")
    for tc in range(NT):
        obuf = tc % 2
        p.wait("s", "v")
        p.wait("s", "t")
        ldo = p.dma("s", s_ost.rearrange("p (c f) -> p c f", f=128),
                    o_spill.rearrange("(c p) t -> p c t", p=128)[:, :, tc * 128:(tc + 1) * 128],
                    sem="ldo%d" % obuf)
        p.wait("t", "s.ldo%d" % obuf, ldo)
        for oh in range(2):
            ps = pA if oh == 0 else pB
            p.wait("t", "v")
            for kc in range(8):
                last = (kc == 7)
                bld = (lambda e, _k=kc, _oh=oh, _p=ps: e.matmul(
                    _p[:], s_ost[:, _k * 128:(_k + 1) * 128],
                    s_wqk[:, _k * 2048 + _oh * 512:_k * 2048 + (_oh + 1) * 512],
                    start=(_k == 0), stop=(_k == 7)))
                tm = p.op("t", bld) if last else p.op0("t", bld)
            p.wait("v", "t", tm)
            p.op("v", lambda e, _oh=oh, _p=ps: e.tensor_tensor(
                s_xn[:, _oh * 512:(_oh + 1) * 512], _p[:],
                s_vbF[:, _oh * 512:(_oh + 1) * 512], ALU.add))
        vmm = p.count["v"]
        p.wait("g", "v", vmm)
        p.dma("g", mha_o[tc * 128:(tc + 1) * 128, :], s_xn[:, 0:1024], sem="smha")
        p.wait("v", "g.smha")
    return p



def build_l0_back():
    p = Prog("l0_back")
    xn = p.inp("xn", [TOK, D])
    mha = p.inp("mha", [TOK, D])
    qe = p.inp("qe", [1024, TOK], BF16)
    kvp = p.inp("kvp", [NCORES, 128, 8320])
    W_all = p.inp("W_all", [NCORES, 1024, 1024], BF16)
    C_all = p.inp("C_all", [8, 1024])
    wg = p.inp("wg", [1024, 8])
    ln1g_row = p.inp("ln1g_row", [1, 1024])
    ln1b_row = p.inp("ln1b_row", [1, 1024])
    w_proj = p.inp("w_proj", [1024, 1024])
    bp_row = p.inp("bp_row", [1, 1024])
    identB = p.inp("identB", [128, 128], BF16)
    identF = p.inp("identF", [128, 128])
    iotam9 = p.inp("iotam9", [128, 8])

    s_o = p.out("s_o", [TOK, D])
    y1_o = p.out("y1_o", [TOK, D])
    moe_o = p.out("moe_o", [TOK, D])
    cnt_o = p.out("cnt_o", [8, 16])
    pref_o = p.out("pref_o", [8, TOK])
    oh_o = p.out("oh_o", [8, TOK], BF16)

    s_id = p.sbuf("s_id", [128, 128], BF16)
    s_idF = p.sbuf("s_idF", [128, 128])
    s_ones = p.sbuf("s_ones", [1, 128], BF16)

    s_cbf = p.sbuf("s_cbf", [8, 1024], BF16)
    s_pb = p.sbuf("s_pb", [128, 1024])
    s_row = p.sbuf("s_row", [1, 1024], BF16)
    s_iot = p.sbuf("s_iot", [128, 8])
    s_wg = p.sbuf("s_wg", [128, 8 * 8])
    s_cnt = p.sbuf("s_cnt", [8, 16])
    s_run = p.sbuf("s_run", [8, 1])
    s_gt = p.sbuf("s_gt", [128, 8])
    s_gs = p.sbuf("s_gs", [128, 8])
    s_gc = p.sbuf("s_gc", [128, 4])
    s_stat = p.sbuf("s_stat", [128, 4])
    s_tile = p.sbuf("s_tile", [128, 1024])
    s_t2 = p.sbuf("s_t2", [128, 1024])
    s_pr = p.sbuf("s_pr", [1, 1024])
    s_onesf = p.sbuf("s_onesf", [1, 128])
    s_ohb = p.sbuf("s_ohb", [128, 8], BF16)
    s_mkb = p.sbuf("s_mkb", [128, 8], BF16)
    pA = p.psum("pA", [128, 512])
    pB = p.psum("pB", [128, 512])
    pG = p.psum("pG", [128, 8])
    pOH = p.psum("pOH", [8, 128], BF16)

    s_wproj = p.sbuf("s_wproj", [128, 8 * 1024], BF16)
    pT = p.psum("pT", [128, 1024], BF16)
    s_kv = p.sbuf("s_kv", [128, 8320])
    s_kvb = p.sbuf("s_kvb", [128, 8320], BF16)
    s_kvstg = p.sbuf("s_kvstg", [128, 8320])
    s_qe = p.sbuf("s_qe", [128, 8 * TOK], BF16)
    s_otm = p.sbuf("s_otm", [128, 1024], BF16)
    s_oF = p.sbuf("s_oF", [128, 8 * 128], BF16)
    s_z = p.sbuf("s_z", [128, 2])
    pO = p.psum("pO", [128, 65])
    pO2 = p.psum("pO2", [128, 65])
    o_evac = {}

    p.dma("s", s_id[:], identB, sem="ld")
    p.dma("s", s_idF[:], identF, sem="ld")
    p.dma("s", s_iot[:], iotam9, sem="ld")
    p.dma("s", s_pr[:], bp_row, sem="ld")
    p.op("g", lambda e: e.memset(s_ones[:], 1.0))
    p.op("g", lambda e: e.memset(s_onesf[:], 1.0))
    p.op("g", lambda e: e.memset(s_run[:], 0.0))
    p.dma("g", s_qe.rearrange("p (c f) -> p c f", f=TOK),
          qe.rearrange("(c p) f -> p c f", p=128), sem="ldqe")
    p.dma("g", s_wproj.rearrange("p (c f) -> p c f", f=1024),
          w_proj.rearrange("(c p) f -> p c f", p=128), sem="ldw")
    p.dma("g", s_wg.rearrange("p (c f) -> p c f", f=8),
          wg.rearrange("(c p) f -> p c f", p=128), sem="ldw")
    p.dma("g", s_cbf[:], C_all, sem="ldw")

    p.wait("t", "s.ld")
    p.wait("t", "g")
    for oh in range(2):
        ps = pA if oh == 0 else pB
        bld = (lambda e, _oh=oh, _p=ps: e.matmul(
            _p[:], s_onesf[:], s_pr[:, _oh * 512:(_oh + 1) * 512], start=True, stop=True))
        tm = p.op("t", bld)
        p.wait("v", "t", tm)
        p.op("v", lambda e, _oh=oh, _p=ps: e.tensor_copy(
            s_pb[:, _oh * 512:(_oh + 1) * 512], _p[:]))
    vmbc = p.count["v"]
    p.wait("t", "v", vmbc)

    lkv = p.dma("s", s_kv[:], kvp[0], sem="ldkv")
    p.wait("v", "s.ldkv", lkv)
    for c in range(1, NCORES):
        lm = p.dma("s", s_kvstg[:], kvp[c], sem="ldkvs")
        p.wait("v", "s.ldkvs", lm)
        vm = p.op("v", lambda e: e.tensor_tensor(s_kv[:], s_kv[:], s_kvstg[:], ALU.add))
        p.wait("s", "v", vm)
    vmc = p.op("v", lambda e: e.tensor_copy(s_kvb[:], s_kv[:]))
    p.wait("t", "v", vmc)
    p.wait("t", "g.ldqe")
    p.wait("t", "g.ldw")

    for b in range(B):
        for half in range(2):
            for h in range(16):
                ch, row = h // 2, (h % 2) * 64
                pair = b * 16 + h
                po = pO if h % 2 == 0 else pO2
                if h >= 2:
                    p.wait("t", "v", o_evac[h - 2])
                bld = (lambda e, _ch=ch, _row=row, _b=b, _hf=half, _pair=pair, _po=po: e.matmul(
                    _po[_row:_row + 64, :] if False else _po[0:128, 0:65],
                    s_qe[_row:_row + 64, :].rearrange("p (c n l) -> p c n l", c=8, l=16)[:, _ch, _hf * 128:(_hf + 1) * 128, _b],
                    s_kvb[_row:_row + 64, (_pair // 2) * 65:(_pair // 2) * 65 + 65],
                    start=True, stop=True))
                tm = p.op("t", bld)
                p.wait("v", "t", tm)
                p.op("v", lambda e, _po=po: e.tensor_scalar(
                    s_z[:, 0:1], _po[:, 64:65], 1e-8, None, ALU.add))
                p.op("v", lambda e: e.reciprocal(s_z[:, 0:1], s_z[:, 0:1]))
                o_evac[h] = p.op("v", lambda e, _h=h, _po=po: e.tensor_scalar(
                    s_otm[:, _h * 64:_h * 64 + 64], _po[:, 0:64], s_z[:, 0:1], None, ALU.mult))
            p.wait("t", "v", o_evac[15])
            for kc2 in range(8):
                bld = (lambda e, _k=kc2: e.transpose(
                    pT[:, _k * 128:(_k + 1) * 128], s_otm[:, _k * 128:(_k + 1) * 128], s_id[:]))
                tm = p.op("t", bld) if kc2 == 7 else p.op0("t", bld)
            p.wait("v", "t", tm)
            vmo = p.op("v", lambda e: e.tensor_copy(
                s_oF.rearrange("p (c f) -> p c f", f=128), pT.rearrange("p (c f) -> p c f", f=128)))
            p.wait("t", "v", vmo)
            for oh in range(2):
                ps = pA if oh == 0 else pB
                for kc2 in range(8):
                    last = (kc2 == 7)
                    bld = (lambda e, _k=kc2, _oh=oh, _p=ps: e.matmul(
                        _p[:], s_oF[:, _k * 128:(_k + 1) * 128],
                        s_wproj[:, _k * 1024 + _oh * 512:_k * 1024 + (_oh + 1) * 512],
                        start=(_k == 0), stop=(_k == 7)))
                    tm = p.op("t", bld) if last else p.op0("t", bld)
                p.wait("v", "t", tm)
                p.op("v", lambda e, _oh=oh, _p=ps: e.tensor_tensor(
                    s_tile[:, _oh * 512:(_oh + 1) * 512], _p[:],
                    s_pb[:, _oh * 512:(_oh + 1) * 512], ALU.add))
            vms = p.count["v"]
            p.wait("g", "v", vms)
            p.dma("g", s_o[(b * 256 + half * 128):(b * 256 + half * 128 + 128), :],
                  s_tile[:, 0:1024], sem="ss")
            p.wait("v", "g.ss")
            p.wait("t", "v", vms)

    NT = TOK // 128
    p.fence()
    p.free_sbuf("pO2", "pO", "s_z", "s_oF", "s_otm", "s_qe", "s_kvstg", "s_kvb", "s_kv",
                "pT", "s_wproj")
    s_y1F = p.sbuf("s_y1F", [128, KCD * TOK])
    s_mF = p.sbuf("s_mF", [8, TOK], BF16)
    s_ohF = p.sbuf("s_ohF", [8, TOK], BF16)
    s_prefF = p.sbuf("s_prefF", [8, TOK])
    s_gr = p.sbuf("s_gr", [1, 1024])
    s_br = p.sbuf("s_br", [1, 1024])
    s_gb = p.sbuf("s_gb", [128, 2 * 1024])
    s_xt = p.sbuf("s_xt", [128, 1024])
    s_mt = p.sbuf("s_mt", [128, 1024])
    pT2 = p.psum("pT2", [128, 1024])
    p.dma("s", s_gr[:], ln1g_row, sem="ld3")
    p.dma("s", s_br[:], ln1b_row, sem="ld3")
    p.wait("t", "s.ld3")
    for i, rowt in enumerate((s_gr, s_br)):
        for oh_ in range(2):
            ps = pA if oh_ == 0 else pB
            bld = (lambda e, _r=rowt, _oh=oh_, _p=ps: e.matmul(
                _p[:], s_onesf[:], _r[:, _oh * 512:(_oh + 1) * 512], start=True, stop=True))
            tm = p.op("t", bld)
            p.wait("v", "t", tm)
            p.op("v", lambda e, _i=i, _oh=oh_, _p=ps: e.tensor_copy(
                s_gb[:, _i * 1024 + _oh * 512:_i * 1024 + (_oh + 1) * 512], _p[:]))
        vmbc = p.count["v"]
        p.wait("t", "v", vmbc)
    inv_d = 1.0 / D
    oh_tp = {}
    for tc in range(NT):
        par = tc % 2
        l1 = p.dma("s", s_xt[:], xn[tc * 128:(tc + 1) * 128, :], sem="lxn")
        l2 = p.dma("s", s_mt[:], mha[tc * 128:(tc + 1) * 128, :], sem="lmh")
        p.wait("v", "s.lxn", l1)
        p.wait("v", "s.lmh", l2)
        p.op("v", lambda e: e.tensor_tensor(s_tile[:, 0:1024], s_xt[:], s_mt[:], ALU.add))
        vgate = p.count["v"]
        if tc + 2 <= NT - 1:
            pass
        p.wait("s", "v", vgate)
        p.op("v", lambda e: e.reduce_sum(s_stat[:, 0:1], s_tile[:, 0:1024], AX.X))
        p.wait("a", "v")
        p.op("a", lambda e: e.activation(s_t2[:, 0:1024], s_tile[:, 0:1024], AF.Square))
        p.wait("v", "a")
        p.op("v", lambda e: e.reduce_sum(s_stat[:, 1:2], s_t2[:, 0:1024], AX.X))
        p.op("v", lambda e: e.tensor_scalar(s_stat[:, 0:1], s_stat[:, 0:1], inv_d, None, ALU.mult))
        p.op("v", lambda e: e.tensor_tensor(s_stat[:, 2:3], s_stat[:, 0:1], s_stat[:, 0:1], ALU.mult))
        p.op("v", lambda e: e.tensor_scalar(s_stat[:, 1:2], s_stat[:, 1:2], inv_d, None, ALU.mult))
        p.op("v", lambda e: e.tensor_tensor(s_stat[:, 1:2], s_stat[:, 1:2], s_stat[:, 2:3], ALU.subtract))
        vv = p.op("v", lambda e: e.tensor_scalar(s_stat[:, 1:2], s_stat[:, 1:2], 1.0, 1e-5, ALU.mult, ALU.add))
        p.wait("a", "v", vv)
        aa = p.op("a", lambda e: e.activation(s_stat[:, 3:4], s_stat[:, 1:2], AF.Sqrt))
        p.wait("v", "a", aa)
        p.op("v", lambda e: e.reciprocal(s_stat[:, 3:4], s_stat[:, 3:4]))
        p.op("v", lambda e: e.tensor_scalar(
            s_tile[:, 0:1024], s_tile[:, 0:1024], s_stat[:, 0:1], s_stat[:, 3:4],
            ALU.subtract, ALU.mult))
        p.op("v", lambda e: e.tensor_tensor(s_tile[:, 0:1024], s_tile[:, 0:1024], s_gb[:, 0:1024], ALU.mult))
        p.op("v", lambda e: e.tensor_tensor(s_tile[:, 0:1024], s_tile[:, 0:1024], s_gb[:, 1024:2048], ALU.add))
        vy = p.count["v"]
        p.wait("g", "v", vy)
        p.dma("g", y1_o[tc * 128:(tc + 1) * 128, :], s_tile[:, 0:1024], sem="sy1")
        p.wait("t", "v", vy)
        for kc2 in range(8):
            bld = (lambda e, _k=kc2: e.transpose(
                pT2[:, _k * 128:(_k + 1) * 128], s_tile[:, _k * 128:(_k + 1) * 128], s_idF[:]))
            tm = p.op("t", bld) if kc2 == 7 else p.op0("t", bld)
        p.wait("v", "t", tm)
        p.wait("v", "g.sy1")
        vf = p.op("v", lambda e, _t=tc: e.tensor_copy(
            s_y1F.rearrange("p (c f) -> p c f", f=TOK)[:, :, _t * 128:(_t + 1) * 128],
            pT2.rearrange("p (c f) -> p c f", f=128)))
        p.wait("t", "v", vf)
        for kc2 in range(8):
            last = (kc2 == 7)
            bld = (lambda e, _k=kc2, _t=tc: e.matmul(
                pG[:], s_y1F[:, _k * TOK + _t * 128:_k * TOK + (_t + 1) * 128],
                s_wg[:, _k * 8:(_k + 1) * 8], start=(_k == 0), stop=(_k == 7)))
            tm = p.op("t", bld) if last else p.op0("t", bld)
        p.wait("v", "t", tm)
        p.op("v", lambda e: e.reduce_max(s_gc[:, 0:1], pG[:], AX.X))
        vv = p.op("v", lambda e: e.tensor_scalar(s_gc[:, 1:2], s_gc[:, 0:1], -1.0, None, ALU.mult))
        p.wait("a", "v", vv)
        aa = p.op("a", lambda e: e.activation(s_gt[:], pG[:], AF.Exp, bias=s_gc[:, 1:2]))
        p.wait("v", "a", aa)
        p.op("v", lambda e: e.reduce_sum(s_gc[:, 2:3], s_gt[:], AX.X))
        p.op("v", lambda e: e.reciprocal(s_gc[:, 2:3], s_gc[:, 2:3]))
        p.op("v", lambda e: e.tensor_scalar(s_gs[:], pG[:], s_gc[:, 0:1], None, ALU.is_ge))
        p.op("v", lambda e: e.tensor_tensor(s_gs[:], s_gs[:], s_iot[:], ALU.mult))
        p.op("v", lambda e: e.tensor_reduce(s_gc[:, 3:4], s_gs[:], AX.X, ALU.min))
        p.op("v", lambda e: e.tensor_scalar(s_ohb[:], s_iot[:], s_gc[:, 3:4], None, ALU.is_equal))
        vm2 = p.op("v", lambda e: e.tensor_scalar(s_mkb[:], s_ohb[:], s_gc[:, 2:3], None, ALU.mult))
        p.wait("t", "v", vm2)
        t1 = p.op("t", lambda e: e.transpose(pOH[:, 0:128], s_ohb[:], s_id[:]))
        p.wait("v", "t", t1)
        vcp = p.op("v", lambda e, _t=tc: e.tensor_copy(s_ohF[:, _t * 128:(_t + 1) * 128], pOH[0:8, 0:128]))
        p.wait("t", "v", vcp)
        t2m = p.op("t", lambda e: e.transpose(pOH[:, 0:128], s_mkb[:], s_id[:]))
        p.wait("v", "t", t2m)
        vlast = p.op("v", lambda e, _t=tc: e.tensor_copy(s_mF[:, _t * 128:(_t + 1) * 128], pOH[0:8, 0:128]))
        p.wait("t", "v", vlast)

    ohv = s_ohF.rearrange("e (n l) -> e l n", l=16)
    prv = s_prefF.rearrange("e (n l) -> e l n", l=16)
    for l in range(16):
        p.op("v", lambda e, _l=l: e.reduce_sum(
            s_cnt.rearrange("e (l o) -> e l o", o=1)[:, _l], ohv[:, _l], AX.X))
    for l in range(16):
        p.op("v", lambda e, _l=l: e.tensor_tensor_scan(
            prv[:, _l], ohv[:, _l], ohv[:, _l], 0.0, ALU.add, ALU.bypass))
    vmc2 = p.count["v"]
    p.wait("g", "v", vmc2)
    mrow_dram = p.dram("mrow_dram", [8, TOK], BF16)
    p.dma("g", mrow_dram, s_mF[:], sem="mrow")
    p.dma("g", cnt_o, s_cnt[:], sem="so")
    p.dma("g", pref_o, s_prefF[:], sem="so")
    p.dma("g", oh_o, s_ohF[:], sem="so")

    p.fence()
    p.free_sbuf("pT2", "s_mt", "s_xt", "s_gb", "s_br", "s_gr")
    s_msk = p.sbuf("s_msk", [128, 8 * 512], BF16)
    s_Wst = p.sbuf("s_Wst", [128, 8192], BF16)
    s_mrow = p.sbuf("s_mrow", [1, 512], BF16)
    s_bc = s_t2
    pE = p.psum("pE", [128, 512])
    wst = lambda par: s_Wst[:]
    wld = {}
    tcons = {}
    for e_ in range(8):
        par = e_ % 2
        if e_ >= 1:
            p.wait("g", "t", tcons[e_ - 1])
        wld[e_] = p.dma("g", wst(par).rearrange("p (c f) -> p c f", f=1024),
                        W_all[e_].rearrange("(c p) f -> p c f", p=128), sem="wst%d" % par)
        for tg in range(8):
            p.wait("s", "t")
            lmr = p.dma("s", s_mrow[:], mrow_dram[e_:e_ + 1, tg * 512:(tg + 1) * 512], sem="lmr")
            p.wait("t", "s.lmr", lmr)
            bld = (lambda e: e.matmul(
                pE[:], s_ones[:], s_mrow[0:1, 0:512],
                start=True, stop=True))
            tm = p.op("t", bld)
            p.wait("v", "t", tm)
            vb = p.op("v", lambda e: e.tensor_copy(s_bc[:, 0:512], pE[:]))
            p.wait("t", "v", vb)
            for kc2 in range(8):
                p.op("v", lambda e, _k=kc2, _tg=tg: e.tensor_tensor(
                    s_msk[:, _k * 512:(_k + 1) * 512],
                    s_y1F[:, _k * TOK + _tg * 512:_k * TOK + _tg * 512 + 512],
                    s_bc[:, 0:512], ALU.mult))
            vmask = p.count["v"]
            p.wait("t", "v", vmask)
            if e_ == 0 and tg == 0:
                p.wait("t", "g.wst0", wld[0])
            elif tg == 0:
                p.wait("t", "g.wst%d" % par, wld[e_])
            for tci in range(4):
                tc = tg * 4 + tci
                for oh in range(2):
                    ps = pA if oh == 0 else pB
                    for kc2 in range(8):
                        last = (kc2 == 7)
                        bld = (lambda e, _k=kc2, _oh=oh, _tci=tci, _p=ps, _wp=par: e.matmul(
                            _p[:], s_msk[:, _k * 512 + _tci * 128:_k * 512 + _tci * 128 + 128],
                            wst(_wp)[:, _k * 1024 + _oh * 512:_k * 1024 + (_oh + 1) * 512],
                            start=(_k == 0), stop=(_k == 7)))
                        tm = p.op("t", bld) if last else p.op0("t", bld)
                    p.wait("v", "t", tm)
                    p.op("v", lambda e, _oh=oh, _p=ps: e.tensor_copy(
                        s_tile[:, _oh * 512:(_oh + 1) * 512], _p[:]))
                vmoe = p.count["v"]
                p.wait("g", "v", vmoe)
                if e_ == 0:
                    p.dma("g", moe_o[tc * 128:(tc + 1) * 128, :], s_tile[:, 0:1024], sem="smoe")
                else:
                    p.dma("g", moe_o[tc * 128:(tc + 1) * 128, :], s_tile[:, 0:1024],
                          sem="smoe", accum_op=ALU.add)
                p.wait("v", "g.smoe")
                p.wait("t", "v")
        tcons[e_] = p.count["t"]
    for tc in range(NT):
        for oh in range(2):
            ps = pA if oh == 0 else pB
            bld = (lambda e, _oh=oh, _tc=tc, _p=ps: e.matmul(
                _p[:], s_mF[:, _tc * 128:(_tc + 1) * 128],
                s_cbf[:, _oh * 512:(_oh + 1) * 512], start=True, stop=True))
            tm = p.op("t", bld)
            p.wait("v", "t", tm)
            p.op("v", lambda e, _oh=oh, _p=ps: e.tensor_copy(
                s_tile[:, _oh * 512:(_oh + 1) * 512], _p[:]))
        vb = p.count["v"]
        p.wait("g", "v", vb)
        p.dma("g", moe_o[tc * 128:(tc + 1) * 128, :], s_tile[:, 0:1024],
              sem="smoe", accum_op=ALU.add)
        p.wait("v", "g.smoe")
        p.wait("t", "v")
    return p




def build_l0_tail():
    p = Prog("l0_tail")
    x = p.inp("x", [TOK, D])
    y1 = p.inp("y1", [TOK, D])
    moe = p.inp("moe", [TOK, D])
    s_in = p.inp("s_in", [TOK, D])
    cnt_all = p.inp("cnt_all", [8, 128])
    pref = p.inp("pref", [8, TOK])
    ohF = p.inp("ohF", [8, TOK], BF16)
    ln2g_row = p.inp("ln2g_row", [1, 1024])
    ln2b_row = p.inp("ln2b_row", [1, 1024])
    mask_below = p.inp("mask_below", [8, 1])
    lind = p.inp("lind", [16, TOK])
    x1_o = p.out("x1_o", [TOK, D])

    s_cnt = p.sbuf("s_cnt", [8, 128])
    s_cntx = p.sbuf("s_cntx", [8, 128])
    s_mb = p.sbuf("s_mb", [8, 1])
    s_ones8 = p.sbuf("s_ones8", [8, 1])
    s_onesf = p.sbuf("s_onesf", [1, 128])
    s_offs = p.sbuf("s_offs", [1, 128])
    s_offsT = p.sbuf("s_offsT", [16, 8])
    s_lind = p.sbuf("s_lind", [16, TOK])
    s_pref = p.sbuf("s_pref", [8, TOK])
    s_oh = p.sbuf("s_oh", [8, TOK], BF16)
    s_keepm = p.sbuf("s_keepm", [8, TOK])
    s_keep = p.sbuf("s_keep", [128, 32])
    s_gb = p.sbuf("s_gb", [128, 2 * 1024])
    s_gr = p.sbuf("s_gr", [1, 1024])
    s_br = p.sbuf("s_br", [1, 1024])
    s_stat = p.sbuf("s_stat", [128, 4])
    s_xt = p.sbuf("s_xt", [128, 2 * 1024])
    s_y1t = p.sbuf("s_y1t", [128, 2 * 1024])
    s_mt = p.sbuf("s_mt", [128, 2 * 1024])
    s_st = p.sbuf("s_st", [128, 2 * 1024])
    s_t1 = p.sbuf("s_t1", [128, 1024])
    s_t2 = p.sbuf("s_t2", [128, 1024])
    pA = p.psum("pA", [128, 512])
    pB = p.psum("pB", [128, 512])
    pO = p.psum("pO", [1, 128])
    pK = p.psum("pK", [128, 32])

    p.dma("s", s_cnt[:], cnt_all, sem="ld")
    p.dma("s", s_mb[:], mask_below, sem="ld")
    p.dma("s", s_pref[:], pref, sem="ld")
    p.dma("s", s_oh[:], ohF, sem="ld")
    p.dma("s", s_gr[:], ln2g_row, sem="ld")
    p.dma("s", s_br[:], ln2b_row, sem="ld")
    p.dma("g", s_lind.rearrange("p (c f) -> p c f", f=1024),
          lind.rearrange("p (c f) -> p c f", f=1024), sem="ldl")
    p.op("g", lambda e: e.memset(s_ones8[:], 1.0))
    p.op("g", lambda e: e.memset(s_onesf[:], 1.0))

    keep_marks = {}
    p.wait("v", "s.ld")
    for e_ in range(8):
        p.op("v", lambda e, _e=e_: e.tensor_tensor_scan(
            s_cntx[:, _e * 16:(_e + 1) * 16], s_cnt[:, _e * 16:(_e + 1) * 16],
            s_cnt[:, _e * 16:(_e + 1) * 16], 0.0, ALU.add, ALU.bypass))
    vx = p.op("v", lambda e: e.tensor_tensor(s_cntx[:], s_cntx[:], s_cnt[:], ALU.subtract))
    p.wait("t", "v", vx)
    p.wait("t", "g")
    p.op0("t", lambda e: e.matmul(pO[:], s_ones8[:], s_cntx[:], start=True, stop=False))
    tm = p.op("t", lambda e: e.matmul(pO[:], s_mb[:], s_cnt[:], start=False, stop=True))
    p.wait("v", "t", tm)
    vo = p.op("v", lambda e: e.tensor_copy(s_offs[:], pO[:]))
    offs_dram = p.dram("offs_dram", [128], F32)
    p.wait("g", "v", vo)
    p.dma("g", offs_dram.rearrange("(o f) -> o f", o=1), s_offs[:], sem="ob")
    p.wait("g", "g.ob")
    p.dma("g", s_offsT[:], offs_dram.rearrange("(e l) -> l e", e=8), sem="ob2",
          allow_slow_non_contiguous=True)
    p.wait("t", "g.ob2")
    p.wait("t", "g.ldl")
    for cch in range(8):
        sl = slice(cch * 512, (cch + 1) * 512)
        ps = pA if cch % 2 == 0 else pB
        if cch >= 2:
            p.wait("t", "v", keep_marks[cch - 2])
        tm = p.op("t", lambda e, _s=sl, _p=ps: e.matmul(
            _p[0:8, :], s_offsT[:], s_lind[:, _s], start=True, stop=True))
        p.wait("v", "t", tm)
        p.op("v", lambda e, _s=sl, _p=ps: e.tensor_tensor(
            s_keepm[:, _s], _p[0:8, :], s_pref[:, _s], ALU.add))
        p.op("v", lambda e, _s=sl: e.tensor_scalar(
            s_keepm[:, _s], s_keepm[:, _s], 4097.0, None, ALU.is_lt))
        keep_marks[cch] = p.op("v", lambda e, _s=sl: e.tensor_tensor(
            s_keepm[:, _s], s_keepm[:, _s], s_oh[:, _s], ALU.mult))
    p.wait("t", "v", keep_marks[7])
    for tc in range(32):
        last = (tc == 31)
        bld = (lambda e, _t=tc: e.matmul(
            pK[:, _t:_t + 1], s_keepm[:, _t * 128:(_t + 1) * 128], s_ones8[:],
            start=True, stop=True))
        tm = p.op("t", bld) if last else p.op0("t", bld)
    p.wait("v", "t", tm)
    vk = p.op("v", lambda e: e.tensor_copy(s_keep[:], pK[:]))

    p.wait("t", "v", vk)
    for i, rowt in enumerate((s_gr, s_br)):
        for oh_ in range(2):
            ps = pA if oh_ == 0 else pB
            tm = p.op("t", lambda e, _r=rowt, _o=oh_, _p=ps: e.matmul(
                _p[:], s_onesf[:], _r[:, _o * 512:(_o + 1) * 512], start=True, stop=True))
            p.wait("v", "t", tm)
            vg = p.op("v", lambda e, _i=i, _o=oh_, _p=ps: e.tensor_copy(
                s_gb[:, _i * 1024 + _o * 512:_i * 1024 + (_o + 1) * 512], _p[:]))
        p.wait("t", "v", vg)

    inv_d = 1.0 / D
    for tc in range(32):
        par = tc % 2
        sl = slice(par * 1024, (par + 1) * 1024)
        gate = p.count["v"]
        p.wait("s", "v", gate)
        l1m = p.dma("s", s_xt[:, sl], x[tc * 128:(tc + 1) * 128, :], sem="lx%d" % par)
        l2m = p.dma("s", s_y1t[:, sl], y1[tc * 128:(tc + 1) * 128, :], sem="ly%d" % par)
        l3m = p.dma("s", s_mt[:, sl], moe[tc * 128:(tc + 1) * 128, :], sem="lm%d" % par)
        l4m = p.dma("s", s_st[:, sl],
                    s_in.rearrange("(l n) d -> l n d", l=16)[:, tc * 8:tc * 8 + 8, :]
                        .rearrange("l n d -> n l d"),
                    sem="ls%d" % par)
        for nm, mk in (("lx", l1m), ("ly", l2m), ("lm", l3m), ("ls", l4m)):
            p.wait("v", "s.%s%d" % (nm, par), mk)
        p.op("v", lambda e, _sl=sl, _t=tc: e.tensor_scalar(
            s_t1[:], s_mt[:, _sl], s_keep[:, _t:_t + 1], None, ALU.mult))
        p.op("v", lambda e, _sl=sl: e.tensor_tensor(s_t1[:], s_t1[:], s_y1t[:, _sl], ALU.add))
        p.op("v", lambda e: e.reduce_sum(s_stat[:, 0:1], s_t1[:], AX.X))
        p.wait("a", "v")
        p.op("a", lambda e: e.activation(s_t2[:], s_t1[:], AF.Square))
        p.wait("v", "a")
        p.op("v", lambda e: e.reduce_sum(s_stat[:, 1:2], s_t2[:], AX.X))
        p.op("v", lambda e: e.tensor_scalar(s_stat[:, 0:1], s_stat[:, 0:1], inv_d, None, ALU.mult))
        p.op("v", lambda e: e.tensor_tensor(s_stat[:, 2:3], s_stat[:, 0:1], s_stat[:, 0:1], ALU.mult))
        p.op("v", lambda e: e.tensor_scalar(s_stat[:, 1:2], s_stat[:, 1:2], inv_d, None, ALU.mult))
        p.op("v", lambda e: e.tensor_tensor(s_stat[:, 1:2], s_stat[:, 1:2], s_stat[:, 2:3], ALU.subtract))
        vv = p.op("v", lambda e: e.tensor_scalar(s_stat[:, 1:2], s_stat[:, 1:2], 1.0, 1e-5, ALU.mult, ALU.add))
        p.wait("a", "v", vv)
        aa = p.op("a", lambda e: e.activation(s_stat[:, 3:4], s_stat[:, 1:2], AF.Sqrt))
        p.wait("v", "a", aa)
        p.op("v", lambda e: e.reciprocal(s_stat[:, 3:4], s_stat[:, 3:4]))
        p.op("v", lambda e: e.tensor_scalar(
            s_t1[:], s_t1[:], s_stat[:, 0:1], s_stat[:, 3:4], ALU.subtract, ALU.mult))
        p.op("v", lambda e: e.tensor_tensor(s_t1[:], s_t1[:], s_gb[:, 0:1024], ALU.mult))
        vv = p.op("v", lambda e: e.tensor_tensor(s_t1[:], s_t1[:], s_gb[:, 1024:2048], ALU.add))
        p.wait("a", "v", vv)
        aa = p.op("a", lambda e: e.activation(s_t2[:], s_t1[:], AF.Silu))
        p.wait("v", "a", aa)
        p.op("v", lambda e, _sl=sl: e.tensor_tensor(s_t2[:], s_t2[:], s_st[:, _sl], ALU.mult))
        vf = p.op("v", lambda e, _sl=sl: e.tensor_tensor(s_t2[:], s_t2[:], s_xt[:, _sl], ALU.add))
        p.wait("g", "v", vf)
        p.dma("g", x1_o[tc * 128:(tc + 1) * 128, :], s_t2[:], sem="sx1")
        p.wait("v", "g.sx1")
    return p



def build_l1_attn():
    p = Prog("l1_attn")
    x1 = p.inp("x1", [TOK, D])
    WqT = p.inp("WqT", [1024, 1024])
    WkT = p.inp("WkT", [1024, 1024])
    WvT = p.inp("WvT", [1024, 1024])
    identB = p.inp("identB", [128, 128], BF16)
    glam = p.inp("glam", [128, 128], BF16)

    xn1_o = p.out("xn1_o", [TOK, D], BF16)
    o_o = p.out("o_o", [1024, TOK], BF16)
    max1_o = p.out("max1_o", [1, 1])

    s_id = p.sbuf("s_id", [128, 128], BF16)
    s_glam = p.sbuf("s_glam", [128, 128], BF16)
    s_xnF = p.sbuf("s_xnF", [128, KCD * TOK], BF16)
    s_wq = p.sbuf("s_wq", [128, 8 * 1024], BF16)
    s_wk = p.sbuf("s_wk", [128, 8 * 1024], BF16)
    s_wv = p.sbuf("s_wv", [128, 8 * 1024], BF16)
    s_xt = p.sbuf("s_xt", [128, 2 * 1024])
    s_xnb = p.sbuf("s_xnb", [128, 1024], BF16)
    s_stat = p.sbuf("s_stat", [128, 4])
    s_mx = p.sbuf("s_mx", [128, 2])
    s_sq = p.sbuf("s_sq", [128, 1024])
    s_q = p.sbuf("s_q", [128, 2 * N], BF16)
    s_k = p.sbuf("s_k", [128, 2 * N], BF16)
    s_g = p.sbuf("s_g", [128, 2 * N])
    s_A = p.sbuf("s_A", [128, 2 * N])
    s_f = p.sbuf("s_f", [128, N])
    s_v = p.sbuf("s_v", [128, 16 * 256], BF16)
    s_eA = p.sbuf("s_eA", [128, 128])
    s_iA = p.sbuf("s_iA", [128, 128])
    s_qA = p.sbuf("s_qA", [128, 128], BF16)
    s_kA = p.sbuf("s_kA", [128, 128], BF16)
    s_kD = p.sbuf("s_kD", [128, 128], BF16)
    s_kDt = p.sbuf("s_kDt", [128, 128], BF16)
    s_wm = p.sbuf("s_wm", [128, 128], BF16)
    s_ec = p.sbuf("s_ec", [128, 6])
    s_S = p.sbuf("s_S", [128, 128])
    s_Ssc = p.sbuf("s_Ssc", [128, 128], BF16)
    s_oF = p.sbuf("s_oF", [128, 2 * N], BF16)
    pA = p.psum("pA", [128, 512])
    pB = p.psum("pB", [128, 512])
    pT = p.psum("pT", [128, 1024], BF16)
    pS = p.psum("pS", [128, 128])
    pO = p.psum("pO", [128, 128])
    pDS = p.psum("pDS", [128, 128])
    pTk = p.psum("pTk", [128, 128], BF16)
    pM = p.psum("pM", [1, 128])

    p.dma("s", s_id[:], identB, sem="ld")
    p.dma("s", s_glam[:], glam, sem="ld")
    p.dma("g", s_wq.rearrange("p (c f) -> p c f", f=1024),
          WqT.rearrange("(c p) f -> p c f", p=128), sem="wq")
    p.dma("g", s_wk.rearrange("p (c f) -> p c f", f=1024),
          WkT.rearrange("(c p) f -> p c f", p=128), sem="wk")
    p.dma("g", s_wv.rearrange("p (c f) -> p c f", f=1024),
          WvT.rearrange("(c p) f -> p c f", p=128), sem="wv")

    NT = TOK // 128
    inv_d = 1.0 / D
    p.op("v", lambda e: e.memset(s_mx[:, 0:1], 0.0))
    p.wait("t", "s.ld")
    for tc in range(NT):
        par = tc % 2
        sl = slice(par * 1024, (par + 1) * 1024)
        gate = p.count["v"]
        p.wait("s", "v", gate)
        lm = p.dma("s", s_xt[:, sl], x1[tc * 128:(tc + 1) * 128, :], sem="lx%d" % par)
        p.wait("v", "s.lx%d" % par, lm)
        p.op("v", lambda e, _sl=sl: e.reduce_sum(s_stat[:, 0:1], s_xt[:, _sl], AX.X))
        p.wait("a", "v")
        p.op("a", lambda e, _sl=sl: e.activation(s_sq[:], s_xt[:, _sl], AF.Square))
        p.wait("v", "a")
        p.op("v", lambda e: e.reduce_sum(s_stat[:, 1:2], s_sq[:], AX.X))
        p.op("v", lambda e: e.tensor_scalar(s_stat[:, 0:1], s_stat[:, 0:1], inv_d, None, ALU.mult))
        p.op("v", lambda e: e.tensor_tensor(s_stat[:, 2:3], s_stat[:, 0:1], s_stat[:, 0:1], ALU.mult))
        p.op("v", lambda e: e.tensor_scalar(s_stat[:, 1:2], s_stat[:, 1:2], inv_d, None, ALU.mult))
        p.op("v", lambda e: e.tensor_tensor(s_stat[:, 1:2], s_stat[:, 1:2], s_stat[:, 2:3], ALU.subtract))
        vv = p.op("v", lambda e: e.tensor_scalar(s_stat[:, 1:2], s_stat[:, 1:2], 1.0, 1e-5, ALU.mult, ALU.add))
        p.wait("a", "v", vv)
        aa = p.op("a", lambda e: e.activation(s_stat[:, 3:4], s_stat[:, 1:2], AF.Sqrt))
        p.wait("v", "a", aa)
        p.op("v", lambda e: e.reciprocal(s_stat[:, 3:4], s_stat[:, 3:4]))
        vb = p.op("v", lambda e, _sl=sl: e.tensor_scalar(
            s_xnb[:], s_xt[:, _sl], s_stat[:, 0:1], s_stat[:, 3:4], ALU.subtract, ALU.mult))
        p.op("v", lambda e: e.reduce_max(s_mx[:, 1:2], s_xnb[:], AX.X, apply_absolute_value=True))
        p.op("v", lambda e: e.tensor_tensor(s_mx[:, 0:1], s_mx[:, 0:1], s_mx[:, 1:2], ALU.max))
        p.wait("g", "v", vb)
        p.dma("g", xn1_o[tc * 128:(tc + 1) * 128, :], s_xnb[:], sem="sxn")
        p.wait("t", "v", vb)
        for kc in range(8):
            bld = (lambda e, _k=kc: e.transpose(
                pT[:, _k * 128:(_k + 1) * 128], s_xnb[:, _k * 128:(_k + 1) * 128], s_id[:]))
            tm = p.op("t", bld) if kc == 7 else p.op0("t", bld)
        p.wait("v", "t", tm)
        p.wait("v", "g.sxn")
        vf = p.op("v", lambda e, _t=tc: e.tensor_copy(
            s_xnF.rearrange("p (c f) -> p c f", f=TOK)[:, :, _t * 128:(_t + 1) * 128],
            pT.rearrange("p (c f) -> p c f", f=128)))
        p.wait("t", "v", vf)
    vmx = p.count["v"]
    p.wait("t", "v", vmx)
    p.op("v", lambda e: e.tensor_copy(s_wm[:, 0:1], s_mx[:, 0:1]))
    vmx = p.count["v"]
    p.wait("t", "v", vmx)
    tmx = p.op("t", lambda e: e.transpose(pTk[0:1, 0:128], s_wm[:, 0:1], s_id[:]))
    p.wait("v", "t", tmx)
    p.op("v", lambda e: e.tensor_copy(s_wm[0:1, 0:128], pTk[0:1, 0:128]))
    p.op("v", lambda e: e.tensor_copy(s_ec[0:1, 0:1].rearrange("p o -> p o") if False else s_stat[0:1, 0:1], s_wm[0:1, 0:1])) if False else None
    p.op("v", lambda e: e.reduce_max(s_stat[0:1, 0:1], s_wm[0:1, 0:128].rearrange("p f -> p f"), AX.X))
    vmx2 = p.count["v"]
    p.wait("g", "v", vmx2)
    p.dma("g", max1_o, s_stat[0:1, 0:1], sem="smx")

    xnF = lambda kc: s_xnF[:, kc * TOK:(kc + 1) * TOK]
    p.wait("t", "g.wq")
    p.wait("t", "g.wk")
    p.wait("t", "g.wv")
    for bl in range(2):
        for hg in range(4):
            for hh in range(2):
                h = hg * 2 + hh
                for tgi in range(4):
                    ps = pA if tgi % 2 == 0 else pB
                    p.wait("t", "a")
                    p.wait("t", "v")
                    for kc in range(8):
                        last = (kc == 7)
                        bld = (lambda e, _k=kc, _h=h, _b=bl, _tg=tgi, _p=ps: e.matmul(
                            _p[:], s_wq[:, _k * 1024 + _h * 128:_k * 1024 + _h * 128 + 128],
                            xnF(_k)[:, _b * 2048 + _tg * 512:_b * 2048 + (_tg + 1) * 512],
                            start=(_k == 0), stop=(_k == 7)))
                        tm = p.op("t", bld) if last else p.op0("t", bld)
                    p.wait("a", "t", tm)
                    p.op("a", lambda e, _hh=hh, _tg=tgi, _p=ps: e.activation(
                        s_q[:, _hh * 2048 + _tg * 512:_hh * 2048 + (_tg + 1) * 512],
                        _p[:], AF.Silu))
                for tgi in range(4):
                    ps = pA if tgi % 2 == 0 else pB
                    p.wait("t", "a")
                    p.wait("t", "v")
                    for kc in range(8):
                        last = (kc == 7)
                        bld = (lambda e, _k=kc, _h=h, _b=bl, _tg=tgi, _p=ps: e.matmul(
                            _p[:], s_wk[:, _k * 1024 + _h * 128:_k * 1024 + _h * 128 + 128],
                            xnF(_k)[:, _b * 2048 + _tg * 512:_b * 2048 + (_tg + 1) * 512],
                            start=(_k == 0), stop=(_k == 7)))
                        tm = p.op("t", bld) if last else p.op0("t", bld)
                    p.wait("a", "t", tm)
                    sl = slice(tgi * 512, (tgi + 1) * 512)
                    p.op("a", lambda e, _s=sl, _p=ps: e.activation(s_f[:, _s], _p[:], AF.Sigmoid))
                    p.wait("v", "a")
                    p.op("v", lambda e, _s=sl, _hh=hh, _tg=tgi: e.tensor_scalar(
                        s_k[:, _hh * 2048 + _tg * 512:_hh * 2048 + (_tg + 1) * 512],
                        s_f[:, _s], -1.0, 1.0, ALU.mult, ALU.add))
                    p.op("a", lambda e, _s=sl: e.activation(s_sq[:, 0:512], s_f[:, _s], AF.Sigmoid))
                    p.op("a", lambda e, _s=sl, _hh=hh, _tg=tgi: e.activation(
                        s_g[:, _hh * 2048 + _tg * 512:_hh * 2048 + (_tg + 1) * 512],
                        s_sq[:, 0:512], AF.Ln))
            for tci in range(16):
                ps = pA if tci % 2 == 0 else pB
                p.wait("t", "v")
                for kc in range(8):
                    last = (kc == 7)
                    bld = (lambda e, _k=kc, _b=bl, _t=tci, _hg=hg, _p=ps: e.matmul(
                        _p[:, 0:256], xnF(_k)[:, (_b * 16 + _t) * 128:(_b * 16 + _t) * 128 + 128],
                        s_wv[:, _k * 1024 + _hg * 256:_k * 1024 + _hg * 256 + 256],
                        start=(_k == 0), stop=(_k == 7)))
                    tm = p.op("t", bld) if last else p.op0("t", bld)
                p.wait("v", "t", tm)
                p.op("v", lambda e, _t=tci, _p=ps: e.tensor_copy(
                    s_v[:, _t * 256:(_t + 1) * 256], _p[:, 0:256]))
            p.wait("v", "a")
            for hh in range(2):
                p.op("v", lambda e, _hh=hh: e.tensor_tensor_scan(
                    s_A[:, _hh * 2048:(_hh + 1) * 2048], s_g[:, _hh * 2048:(_hh + 1) * 2048],
                    s_g[:, _hh * 2048:(_hh + 1) * 2048], 0.0, ALU.add, ALU.bypass))
            for hh in range(2):
                hoff = hh * 2048
                for cch in range(16):
                    co = hoff + cch * 128
                    p.op("v", lambda e, _co=co: e.tensor_tensor(
                        s_ec[:, 1:2], s_A[:, _co + 127:_co + 128], s_A[:, _co + 63:_co + 64], ALU.subtract))
                    if cch == 0:
                        p.op("v", lambda e, _co=co: e.tensor_copy(s_ec[:, 0:1], s_A[:, _co + 63:_co + 64]))
                        p.op("v", lambda e, _co=co: e.tensor_copy(s_ec[:, 2:3], s_A[:, _co + 127:_co + 128]))
                    else:
                        p.op("v", lambda e, _co=co: e.tensor_tensor(
                            s_ec[:, 0:1], s_A[:, _co + 63:_co + 64], s_A[:, _co - 1:_co], ALU.subtract))
                        p.op("v", lambda e, _co=co: e.tensor_tensor(
                            s_ec[:, 2:3], s_A[:, _co + 127:_co + 128], s_A[:, _co - 1:_co], ALU.subtract))
                    vv = p.op("v", lambda e, _co=co: e.tensor_scalar(
                        s_eA[:], s_A[:, _co:_co + 128], s_A[:, _co + 63:_co + 64], None, ALU.subtract))
                    p.wait("a", "v", vv)
                    p.op("a", lambda e: e.activation(s_eA[:], s_eA[:], AF.Exp))
                    aa = p.op("a", lambda e: e.activation(s_ec[:, 3:6], s_ec[:, 0:3], AF.Exp))
                    p.wait("v", "a", aa)
                    p.op("v", lambda e: e.reciprocal(s_iA[:], s_eA[:]))
                    p.op("v", lambda e, _co=co: e.tensor_tensor(
                        s_qA[:], s_q[:, _co:_co + 128], s_eA[:], ALU.mult))
                    p.op("v", lambda e, _co=co: e.tensor_tensor(
                        s_kA[:], s_k[:, _co:_co + 128], s_iA[:], ALU.mult))
                    p.op("v", lambda e: e.tensor_scalar(s_kD[:], s_kA[:], s_ec[:, 4:5], None, ALU.mult))
                    if cch > 0:
                        p.op("v", lambda e: e.tensor_scalar(s_Ssc[:], s_S[:], s_ec[:, 3:4], None, ALU.mult))
                    vv = p.count["v"]
                    p.wait("t", "v", vv)
                    tm = p.op("t", lambda e: e.matmul(pS[:], s_kA[:], s_qA[:], start=True, stop=True))
                    p.wait("v", "t", tm)
                    vw = p.op("v", lambda e: e.tensor_tensor(s_wm[:], pS[:], s_glam[:], ALU.mult))
                    p.wait("t", "v", vw)
                    tk = p.op("t", lambda e: e.transpose(pTk[:], s_kD[:], s_id[:]))
                    p.wait("v", "t", tk)
                    vk = p.op("v", lambda e: e.tensor_copy(s_kDt[:], pTk[:]))
                    p.wait("t", "v", vk)
                    tcv = cch
                    if cch == 0:
                        tmo = p.op("t", lambda e, _t=tcv, _hh=hh: e.matmul(
                            pO[:], s_v[:, _t * 256 + _hh * 128:_t * 256 + _hh * 128 + 128],
                            s_wm[:], start=True, stop=True))
                    else:
                        p.op0("t", lambda e, _t=tcv, _hh=hh: e.matmul(
                            pO[:], s_v[:, _t * 256 + _hh * 128:_t * 256 + _hh * 128 + 128],
                            s_wm[:], start=True, stop=False))
                        tmo = p.op("t", lambda e: e.matmul(pO[:], s_Ssc[:], s_qA[:], start=False, stop=True))
                    tds = p.op("t", lambda e, _t=tcv, _hh=hh: e.matmul(
                        pDS[:], s_kDt[:], s_v[:, _t * 256 + _hh * 128:_t * 256 + _hh * 128 + 128],
                        start=True, stop=True))
                    p.wait("v", "t", tds)
                    if cch == 0:
                        p.op("v", lambda e: e.tensor_copy(s_S[:], pDS[:]))
                    else:
                        p.op("v", lambda e: e.scalar_tensor_tensor(
                            s_S[:], s_S[:], s_ec[:, 5:6], pDS[:], ALU.mult, ALU.add))
                    vo = p.op("v", lambda e, _hh=hh, _c=cch: e.tensor_copy(
                        s_oF[:, _hh * 2048 + _c * 128:_hh * 2048 + (_c + 1) * 128], pO[:]))
                    p.wait("t", "v", vo)
            vm = p.count["v"]
            p.wait("g", "v", vm)
            for hh in range(2):
                h = hg * 2 + hh
                p.dma("g", o_o[h * 128:(h + 1) * 128, bl * 2048:(bl + 1) * 2048],
                      s_oF[:, hh * 2048:(hh + 1) * 2048], sem="so")
            if not (bl == 1 and hg == 3):
                p.wait("v", "g.so")
    return p


def emit_grid(p, out_ap, in_ap, inv_col, s_col, s_int, s_r, t1, w, to_bf=True):
    p.op("v", lambda e: e.tensor_scalar(t1[:, 0:w], in_ap, inv_col, 0.5, ALU.mult, ALU.add))
    p.op("v", lambda e: e.tensor_copy(s_int[:, 0:w], t1[:, 0:w]))
    p.op("v", lambda e: e.tensor_copy(s_r[:, 0:w], s_int[:, 0:w]))
    p.op("v", lambda e: e.tensor_tensor(t1[:, 0:w], t1[:, 0:w], s_r[:, 0:w], ALU.is_lt))
    p.op("v", lambda e: e.tensor_tensor(s_r[:, 0:w], s_r[:, 0:w], t1[:, 0:w], ALU.subtract))
    return p.op("v", lambda e: e.tensor_scalar(out_ap, s_r[:, 0:w], s_col, None, ALU.mult))


def emit_global_scale(p, s_sc, mx_row, n, qmax, s_onesf, pA, pB):
    p.op("v", lambda e: e.reduce_max(mx_row[0:1, n:n + 1], mx_row[0:1, 0:n], AX.X))
    p.op("v", lambda e: e.tensor_scalar(
        mx_row[0:1, n:n + 1], mx_row[0:1, n:n + 1], 1.0 / qmax, 1e-12, ALU.mult, ALU.max))
    vv = p.op("v", lambda e: e.reciprocal(mx_row[0:1, n + 1:n + 2], mx_row[0:1, n:n + 1]))
    p.wait("t", "v", vv)
    t1 = p.op("t", lambda e: e.matmul(pA[:, 0:1], s_onesf[:], mx_row[0:1, n:n + 1], start=True, stop=True))
    t2 = p.op("t", lambda e: e.matmul(pB[:, 0:1], s_onesf[:], mx_row[0:1, n + 1:n + 2], start=True, stop=True))
    p.wait("v", "t", t2)
    p.op("v", lambda e: e.tensor_copy(s_sc[:, 0:1], pA[:, 0:1]))
    vv = p.op("v", lambda e: e.tensor_copy(s_sc[:, 1:2], pB[:, 0:1]))
    p.wait("t", "v", vv)
    return vv



def build_l1_up():
    p = Prog("l1_up")
    xn1 = p.inp("xn1", [TOK, D], BF16)
    max1 = p.inp("max1", [1, 8])
    wuT = p.inp("wuT", [1024, 3072], BF16)
    wgT = p.inp("wgT", [1024, 3072], BF16)
    identB = p.inp("identB", [128, 128], BF16)
    up_o = p.out("up_o", [3072, TOK])
    gate_o = p.out("gate_o", [3072, TOK])
    maxug_o = p.out("maxug_o", [1, 2])

    s_id = p.sbuf("s_id", [128, 128], BF16)
    s_onesf = p.sbuf("s_onesf", [1, 128])
    s_mx = p.sbuf("s_mx", [1, 12])
    s_sc = p.sbuf("s_sc", [128, 2])
    s_xqF = p.sbuf("s_xqF", [128, KCD * TOK], BF16)
    s_wu = p.sbuf("s_wu", [128, 8 * 3072], BF16)
    s_wg = p.sbuf("s_wg", [128, 8 * 3072], BF16)
    s_xt = p.sbuf("s_xt", [128, 2 * 1024], BF16)
    s_t1 = p.sbuf("s_t1", [128, 1024])
    s_int = p.sbuf("s_int", [128, 1024], I32)
    s_r = p.sbuf("s_r", [128, 1024])
    s_xqb = p.sbuf("s_xqb", [128, 1024], BF16)
    s_ev = p.sbuf("s_ev", [128, 1024])
    s_mxc = p.sbuf("s_mxc", [128, 4])
    pA = p.psum("pA", [128, 512])
    pB = p.psum("pB", [128, 512])
    pT = p.psum("pT", [128, 1024], BF16)
    pM = p.psum("pM", [2, 128], BF16)

    p.dma("s", s_id[:], identB, sem="ld")
    p.dma("s", s_mx[0:1, 0:8], max1, sem="ld")
    p.dma("g", s_wu.rearrange("p (c f) -> p c f", f=3072),
          wuT.rearrange("(c p) f -> p c f", p=128), sem="lw")
    p.dma("g", s_wg.rearrange("p (c f) -> p c f", f=3072),
          wgT.rearrange("(c p) f -> p c f", p=128), sem="lw")
    p.op("g", lambda e: e.memset(s_onesf[:], 1.0))
    p.wait("v", "s.ld")
    p.wait("v", "g")
    p.wait("t", "g")
    p.wait("t", "g.lw")
    p.wait("t", "s.ld")
    emit_global_scale(p, s_sc, s_mx, 8, 32767.0, s_onesf, pA, pB)

    NT = TOK // 128
    for tc in range(NT):
        par = tc % 2
        sl = slice(par * 1024, (par + 1) * 1024)
        gate = p.count["v"]
        p.wait("s", "v", gate)
        lm = p.dma("s", s_xt[:, sl], xn1[tc * 128:(tc + 1) * 128, :], sem="lx%d" % par)
        p.wait("v", "s.lx%d" % par, lm)
        emit_grid(p, s_xqb[:], s_xt[:, sl], s_sc[:, 1:2], s_sc[:, 0:1],
                  s_int, s_r, s_t1, 1024)
        vq = p.count["v"]
        p.wait("t", "v", vq)
        for kc in range(8):
            bld = (lambda e, _k=kc: e.transpose(
                pT[:, _k * 128:(_k + 1) * 128], s_xqb[:, _k * 128:(_k + 1) * 128], s_id[:]))
            tm = p.op("t", bld) if kc == 7 else p.op0("t", bld)
        p.wait("v", "t", tm)
        vf = p.op("v", lambda e, _t=tc: e.tensor_copy(
            s_xqF.rearrange("p (c f) -> p c f", f=TOK)[:, :, _t * 128:(_t + 1) * 128],
            pT.rearrange("p (c f) -> p c f", f=128)))
        p.wait("t", "v", vf)
    xqF = lambda kc: s_xqF[:, kc * TOK:(kc + 1) * TOK]

    p.op("v", lambda e: e.memset(s_mxc[:, 0:2], 0.0))
    for mi, (wbuf, dst) in enumerate(((s_wu, up_o), (s_wg, gate_o))):
        for oc in range(24):
            for tg in range(8):
                ps = pA if tg % 2 == 0 else pB
                p.wait("t", "v")
                for kc in range(8):
                    last = (kc == 7)
                    bld = (lambda e, _k=kc, _o=oc, _t=tg, _p=ps, _w=wbuf: e.matmul(
                        _p[:], _w[:, _k * 3072 + _o * 128:_k * 3072 + _o * 128 + 128],
                        xqF(_k)[:, _t * 512:(_t + 1) * 512],
                        start=(_k == 0), stop=(_k == 7)))
                    tm = p.op("t", bld) if last else p.op0("t", bld)
                p.wait("v", "t", tm)
                p.op("v", lambda e, _t=tg, _p=ps: e.tensor_copy(
                    s_ev[:, 0:512], _p[:]))
                p.op("v", lambda e, _mi=mi: e.reduce_max(
                    s_mxc[:, 2 + _mi:3 + _mi], s_ev[:, 0:512], AX.X, apply_absolute_value=True))
                vm = p.op("v", lambda e, _mi=mi: e.tensor_tensor(
                    s_mxc[:, _mi:_mi + 1], s_mxc[:, _mi:_mi + 1], s_mxc[:, 2 + _mi:3 + _mi], ALU.max))
                p.wait("g", "v", vm)
                p.dma("g", dst[oc * 128:(oc + 1) * 128, tg * 512:(tg + 1) * 512],
                      s_ev[:, 0:512], sem="sug")
                p.wait("v", "g.sug")
    vv = p.op("v", lambda e: e.tensor_copy(s_xqb[:, 0:2], s_mxc[:, 0:2]))
    p.wait("t", "v", vv)
    tm = p.op("t", lambda e: e.transpose(pM[0:2, 0:128], s_xqb[:, 0:2], s_id[:]))
    p.wait("v", "t", tm)
    p.op("v", lambda e: e.tensor_copy(s_t1[0:2, 0:128], pM[0:2, 0:128]))
    vv = p.op("v", lambda e: e.reduce_max(s_t1[0:2, 128:129], s_t1[0:2, 0:128], AX.X))
    mx_dram = p.dram("mx_dram", [2, 1], F32)
    p.wait("g", "v", vv)
    p.dma("g", mx_dram, s_t1[0:2, 128:129], sem="mx1")
    p.wait("g", "g.mx1")
    p.dma("g", maxug_o, mx_dram.rearrange("a b -> b a"), sem="mx2")
    return p



def emit_scale_from_row(p, s_sc, s_mxrow, red_ap, qmax, s_onesf, pA, pB, c0, c1):
    p.op("v", lambda e: e.reduce_max(s_mxrow[0:1, 30:31], red_ap, AX.X))
    p.op("v", lambda e: e.tensor_scalar(
        s_mxrow[0:1, 30:31], s_mxrow[0:1, 30:31], 1.0 / qmax, 1e-12, ALU.mult, ALU.max))
    vv = p.op("v", lambda e: e.reciprocal(s_mxrow[0:1, 31:32], s_mxrow[0:1, 30:31]))
    p.wait("t", "v", vv)
    p.op0("t", lambda e: e.matmul(pA[:, 0:1], s_onesf[:], s_mxrow[0:1, 30:31], start=True, stop=True))
    t2 = p.op("t", lambda e: e.matmul(pB[:, 0:1], s_onesf[:], s_mxrow[0:1, 31:32], start=True, stop=True))
    p.wait("v", "t", t2)
    p.op("v", lambda e, _c=c0: e.tensor_copy(s_sc[:, _c:_c + 1], pA[:, 0:1]))
    vv = p.op("v", lambda e, _c=c1: e.tensor_copy(s_sc[:, _c:_c + 1], pB[:, 0:1]))
    p.wait("t", "v", vv)
    return vv


def build_l1_mid():
    p = Prog("l1_mid")
    up = p.inp("up", [3072, TOK])
    gate = p.inp("gate", [3072, TOK])
    maxug = p.inp("maxug", [1, 16])
    identB = p.inp("identB", [128, 128], BF16)
    prod_o = p.out("prod_o", [3072, TOK])
    upq_o = p.out("upq_o", [3072, TOK])
    maxp_o = p.out("maxp_o", [1, 1])

    s_onesf = p.sbuf("s_onesf", [1, 128])
    s_id = p.sbuf("s_id", [128, 128], BF16)
    s_mx = p.sbuf("s_mx", [1, 32])
    s_sc = p.sbuf("s_sc", [128, 4])
    s_u = p.sbuf("s_u", [128, 2 * 2048])
    s_g = p.sbuf("s_g", [128, 2 * 2048])
    s_t1 = p.sbuf("s_t1", [128, 2048])
    s_int = p.sbuf("s_int", [128, 2048], I32)
    s_r = p.sbuf("s_r", [128, 2048])
    s_sig = p.sbuf("s_sig", [128, 2048])
    s_out = p.sbuf("s_out", [128, 2048])
    s_mxc = p.sbuf("s_mxc", [128, 2])
    s_mxb = p.sbuf("s_mxb", [128, 1], BF16)
    pA = p.psum("pA", [128, 512])
    pB = p.psum("pB", [128, 512])
    pM = p.psum("pM", [1, 128], BF16)

    p.dma("s", s_mx[0:1, 0:16], maxug, sem="ld")
    p.dma("s", s_id[:], identB, sem="ld")
    p.op("g", lambda e: e.memset(s_onesf[:], 1.0))
    p.wait("v", "s.ld")
    p.wait("v", "g")
    p.wait("t", "s.ld")
    p.wait("t", "g")
    emit_scale_from_row(p, s_sc, s_mx, s_mx[0:1, 0:16].rearrange("o (c i) -> o c i", i=2)[:, :, 0],
                        32767.0, s_onesf, pA, pB, 0, 1)
    emit_scale_from_row(p, s_sc, s_mx, s_mx[0:1, 0:16].rearrange("o (c i) -> o c i", i=2)[:, :, 1],
                        32767.0, s_onesf, pA, pB, 2, 3)
    p.op("v", lambda e: e.memset(s_mxc[:, 0:1], 0.0))

    for oc in range(24):
        for hf in range(2):
            par = hf
            usl = slice(par * 2048, (par + 1) * 2048)
            gate_v = p.count["v"]
            p.wait("s", "v", gate_v)
            l1 = p.dma("s", s_u[:, usl], up[oc * 128:(oc + 1) * 128, hf * 2048:(hf + 1) * 2048],
                       sem="lu%d" % par)
            l2 = p.dma("s", s_g[:, usl], gate[oc * 128:(oc + 1) * 128, hf * 2048:(hf + 1) * 2048],
                       sem="lg%d" % par)
            p.wait("v", "s.lu%d" % par, l1)
            p.wait("v", "s.lg%d" % par, l2)
            emit_grid(p, s_u[:, usl], s_u[:, usl], s_sc[:, 1:2], s_sc[:, 0:1],
                      s_int, s_r, s_t1, 2048)
            emit_grid(p, s_g[:, usl], s_g[:, usl], s_sc[:, 3:4], s_sc[:, 2:3],
                      s_int, s_r, s_t1, 2048)
            vv = p.count["v"]
            p.wait("a", "v", vv)
            aa = p.op("a", lambda e, _s=usl: e.activation(s_sig[:], s_g[:, _s], AF.Sigmoid))
            p.wait("v", "a", aa)
            p.op("v", lambda e: e.tensor_scalar(s_t1[:], s_sig[:], 65536.0, 0.5, ALU.mult, ALU.add))
            p.op("v", lambda e: e.tensor_copy(s_int[:], s_t1[:]))
            p.op("v", lambda e: e.tensor_copy(s_r[:], s_int[:]))
            p.op("v", lambda e: e.tensor_tensor(s_t1[:], s_t1[:], s_r[:], ALU.is_lt))
            p.op("v", lambda e: e.tensor_tensor(s_r[:], s_r[:], s_t1[:], ALU.subtract))
            p.op("v", lambda e: e.tensor_scalar(s_sig[:], s_r[:], 65535.0, None, ALU.min))
            p.op("v", lambda e, _s=usl: e.tensor_tensor(s_out[:], s_g[:, _s], s_sig[:], ALU.mult))
            p.op("v", lambda e: e.tensor_scalar(s_out[:], s_out[:], 1.0 / 65536.0, None, ALU.mult))
            p.op("v", lambda e: e.reduce_max(s_mxc[:, 1:2], s_out[:], AX.X, apply_absolute_value=True))
            vv = p.op("v", lambda e: e.tensor_tensor(s_mxc[:, 0:1], s_mxc[:, 0:1], s_mxc[:, 1:2], ALU.max))
            p.wait("g", "v", vv)
            p.dma("g", prod_o[oc * 128:(oc + 1) * 128, hf * 2048:(hf + 1) * 2048],
                  s_out[:], sem="sp")
            p.dma("g", upq_o[oc * 128:(oc + 1) * 128, hf * 2048:(hf + 1) * 2048],
                  s_u[:, usl], sem="sp")
            p.wait("v", "g.sp")
    vv = p.op("v", lambda e: e.tensor_copy(s_mxb[:], s_mxc[:, 0:1]))
    p.wait("t", "v", vv)
    tm = p.op("t", lambda e: e.transpose(pM[0:1, 0:128], s_mxb[:], s_id[:]))
    p.wait("v", "t", tm)
    p.op("v", lambda e: e.tensor_copy(s_t1[0:1, 0:128], pM[0:1, 0:128]))
    vv = p.op("v", lambda e: e.reduce_max(s_t1[0:1, 128:129], s_t1[0:1, 0:128], AX.X))
    p.wait("g", "v", vv)
    p.dma("g", maxp_o, s_t1[0:1, 128:129], sem="smx")
    return p



def build_l1_mid2():
    p = Prog("l1_mid2")
    prod = p.inp("prod", [3072, TOK])
    upq = p.inp("upq", [3072, TOK])
    maxp = p.inp("maxp", [1, 8])
    identB = p.inp("identB", [128, 128], BF16)
    prod2_o = p.out("prod2_o", [3072, TOK])
    maxp2_o = p.out("maxp2_o", [1, 1])

    s_onesf = p.sbuf("s_onesf", [1, 128])
    s_id = p.sbuf("s_id", [128, 128], BF16)
    s_mx = p.sbuf("s_mx", [1, 32])
    s_sc = p.sbuf("s_sc", [128, 2])
    s_u = p.sbuf("s_u", [128, 2 * 2048])
    s_g = p.sbuf("s_g", [128, 2 * 2048])
    s_t1 = p.sbuf("s_t1", [128, 2048])
    s_int = p.sbuf("s_int", [128, 2048], I32)
    s_r = p.sbuf("s_r", [128, 2048])
    s_mxc = p.sbuf("s_mxc", [128, 2])
    s_mxb = p.sbuf("s_mxb", [128, 1], BF16)
    pA = p.psum("pA", [128, 512])
    pB = p.psum("pB", [128, 512])
    pM = p.psum("pM", [1, 128], BF16)

    p.dma("s", s_mx[0:1, 0:8], maxp, sem="ld")
    p.dma("s", s_id[:], identB, sem="ld")
    p.op("g", lambda e: e.memset(s_onesf[:], 1.0))
    p.wait("v", "s.ld")
    p.wait("v", "g")
    p.wait("t", "s.ld")
    p.wait("t", "g")
    emit_scale_from_row(p, s_sc, s_mx, s_mx[0:1, 0:8], 32767.0, s_onesf, pA, pB, 0, 1)
    p.op("v", lambda e: e.memset(s_mxc[:, 0:1], 0.0))

    for oc in range(24):
        for hf in range(2):
            par = hf
            usl = slice(par * 2048, (par + 1) * 2048)
            gate_v = p.count["v"]
            p.wait("s", "v", gate_v)
            l1 = p.dma("s", s_g[:, usl], prod[oc * 128:(oc + 1) * 128, hf * 2048:(hf + 1) * 2048],
                       sem="lp%d" % par)
            l2 = p.dma("s", s_u[:, usl], upq[oc * 128:(oc + 1) * 128, hf * 2048:(hf + 1) * 2048],
                       sem="lu%d" % par)
            p.wait("v", "s.lp%d" % par, l1)
            p.wait("v", "s.lu%d" % par, l2)
            emit_grid(p, s_g[:, usl], s_g[:, usl], s_sc[:, 1:2], s_sc[:, 0:1],
                      s_int, s_r, s_t1, 2048)
            p.op("v", lambda e, _s=usl: e.tensor_tensor(s_g[:, _s], s_g[:, _s], s_u[:, _s], ALU.mult))
            p.op("v", lambda e, _s=usl: e.reduce_max(s_mxc[:, 1:2], s_g[:, _s], AX.X, apply_absolute_value=True))
            vv = p.op("v", lambda e: e.tensor_tensor(s_mxc[:, 0:1], s_mxc[:, 0:1], s_mxc[:, 1:2], ALU.max))
            p.wait("g", "v", vv)
            p.dma("g", prod2_o[oc * 128:(oc + 1) * 128, hf * 2048:(hf + 1) * 2048],
                  s_g[:, usl], sem="sp")
            p.wait("v", "g.sp")
    vv = p.op("v", lambda e: e.tensor_copy(s_mxb[:], s_mxc[:, 0:1]))
    p.wait("t", "v", vv)
    tm = p.op("t", lambda e: e.transpose(pM[0:1, 0:128], s_mxb[:], s_id[:]))
    p.wait("v", "t", tm)
    p.op("v", lambda e: e.tensor_copy(s_t1[0:1, 0:128], pM[0:1, 0:128]))
    vv = p.op("v", lambda e: e.reduce_max(s_t1[0:1, 128:129], s_t1[0:1, 0:128], AX.X))
    p.wait("g", "v", vv)
    p.dma("g", maxp2_o, s_t1[0:1, 128:129], sem="smx")
    return p



def build_l1_tail():
    p = Prog("l1_tail")
    prod2 = p.inp("prod2", [3072, TOK])
    maxp2 = p.inp("maxp2", [1, 8])
    o_gla = p.inp("o_gla", [1024, TOK], BF16)
    x1 = p.inp("x1", [TOK, D])
    wdT = p.inp("wdT", [3072, 1024], BF16)
    WoT = p.inp("WoT", [1024, 1024])
    normw_col = p.inp("normw_col", [128, 8])
    identB = p.inp("identB", [128, 128], BF16)
    out_o = p.out("out_o", [TOK, D])
    dbg_s1 = p.out("dbg_s1", [TOK, D])
    dbg_g1 = p.out("dbg_g1", [TOK, D])

    s_onesf = p.sbuf("s_onesf", [1, 128])
    s_id = p.sbuf("s_id", [128, 128], BF16)
    s_mx = p.sbuf("s_mx", [1, 32])
    s_sc = p.sbuf("s_sc", [128, 2])
    s_wd = p.sbuf("s_wd", [128, 24 * 1024], BF16)
    s_wo = p.sbuf("s_wo", [128, 8 * 1024], BF16)
    s_nw = p.sbuf("s_nw", [128, 8])
    s_p2 = p.sbuf("s_p2", [128, 2 * 3072])
    s_oq = p.sbuf("s_oq", [128, 2 * 3072], BF16)
    s_og = p.sbuf("s_og", [128, 2 * 1024], BF16)
    s_osq = p.sbuf("s_osq", [128, 1024])
    s_t1 = p.sbuf("s_t1", [128, 3072])
    s_int = p.sbuf("s_int", [128, 3072], I32)
    s_r = p.sbuf("s_r", [128, 3072])
    s_g1 = p.sbuf("s_g1", [128, 1024])
    s_s1 = p.sbuf("s_s1", [128, 1024])
    s_x = p.sbuf("s_x", [128, 2 * 1024])
    s_rst = p.sbuf("s_rst", [1, 256])
    s_rstb = p.sbuf("s_rstb", [1, 128], BF16)
    s_rstc = p.sbuf("s_rstc", [128, 1])
    s_onescol = p.sbuf("s_onescol", [128, 1])
    pA = p.psum("pA", [128, 512])
    pB = p.psum("pB", [128, 512])
    pC = p.psum("pC", [128, 512])
    pD = p.psum("pD", [128, 512])
    pR = p.psum("pR", [1, 1024])
    pRT = p.psum("pRT", [128, 1], BF16)

    p.dma("s", s_mx[0:1, 0:8], maxp2, sem="ld")
    p.dma("s", s_id[:], identB, sem="ld")
    p.dma("s", s_nw[:], normw_col, sem="ld")
    p.dma("g", s_wd.rearrange("p (c f) -> p c f", f=1024),
          wdT.rearrange("(c p) f -> p c f", p=128), sem="lw")
    p.dma("g", s_wo.rearrange("p (c f) -> p c f", f=1024),
          WoT.rearrange("(c p) f -> p c f", p=128), sem="lw")
    p.op("g", lambda e: e.memset(s_onesf[:], 1.0))
    p.op("g", lambda e: e.memset(s_onescol[:], 1.0))
    p.wait("v", "s.ld")
    p.wait("v", "g")
    p.wait("v", "g.lw")
    p.wait("t", "s.ld")
    p.wait("t", "g")
    p.wait("t", "g.lw")
    emit_scale_from_row(p, s_sc, s_mx, s_mx[0:1, 0:8], 32767.0, s_onesf, pA, pB, 0, 1)
    for kc in range(8):
        p.op("v", lambda e, _k=kc: e.tensor_scalar(
            s_wo[:, _k * 1024:(_k + 1) * 1024], s_wo[:, _k * 1024:(_k + 1) * 1024],
            s_nw[:, _k:_k + 1], None, ALU.mult))
    vwo = p.count["v"]
    p.wait("t", "v", vwo)

    NT = TOK // 128
    inv_d = 1.0 / D
    for tc in range(NT):
        par = tc % 2
        p2s = slice(par * 3072, (par + 1) * 3072)
        ogs = slice(par * 1024, (par + 1) * 1024)
        gate_v = p.count["v"]
        tgate = p.count["t"]
        p.wait("s", "v", gate_v)
        p.wait("s", "t", tgate)
        l1 = p.dma("s", s_p2[:, p2s].rearrange("p (c f) -> p c f", f=128),
                   prod2.rearrange("(c p) t -> p c t", p=128)[:, :, tc * 128:(tc + 1) * 128],
                   sem="lp%d" % par)
        l2 = p.dma("s", s_og[:, ogs].rearrange("p (c f) -> p c f", f=128),
                   o_gla.rearrange("(c p) t -> p c t", p=128)[:, :, tc * 128:(tc + 1) * 128],
                   sem="lo%d" % par)
        l3 = p.dma("s", s_x[:, ogs], x1[tc * 128:(tc + 1) * 128, :], sem="lx%d" % par)
        p.wait("v", "s.lp%d" % par, l1)
        emit_grid(p, s_oq[:, p2s], s_p2[:, p2s], s_sc[:, 1:2], s_sc[:, 0:1],
                  s_int, s_r, s_t1, 3072)
        voq = p.count["v"]
        p.wait("t", "v", voq)
        for oh in range(2):
            ps = pA if oh == 0 else pB
            for kc in range(24):
                last = (kc == 23)
                bld = (lambda e, _k=kc, _oh=oh, _p=ps, _s=p2s: e.matmul(
                    _p[:], s_oq[:, _s.start + _k * 128:_s.start + (_k + 1) * 128],
                    s_wd[:, _k * 1024 + _oh * 512:_k * 1024 + (_oh + 1) * 512],
                    start=(_k == 0), stop=(_k == 23)))
                tm = p.op("t", bld) if last else p.op0("t", bld)
            p.wait("a", "t", tm)
            p.op("a", lambda e, _oh=oh, _p=ps: e.activation(
                s_g1[:, _oh * 512:(_oh + 1) * 512], _p[:], AF.Silu))
        p.wait("a", "s.lo%d" % par, l2)
        aa = p.op("a", lambda e, _s=ogs: e.activation(s_osq[:], s_og[:, _s], AF.Square))
        p.wait("t", "a", aa)
        tm = None
        for oh in range(2):
            bld = (lambda e, _oh=oh: e.matmul(
                pR[:, _oh * 512:(_oh + 1) * 512], s_onescol[:],
                s_osq[:, _oh * 512:(_oh + 1) * 512], start=True, stop=True))
            tm = p.op("t", bld)
        p.wait("v", "t", tm)
        p.op("v", lambda e: e.reduce_sum(
            s_rst[0:1, 0:128].rearrange("o (f c) -> o f c", c=1),
            pR[0:1, 0:1024].rearrange("o (c f) -> o f c", f=128), AX.X))
        p.op("v", lambda e: e.tensor_scalar(s_rst[0:1, 0:128], s_rst[0:1, 0:128], inv_d, 1e-6, ALU.mult, ALU.add))
        vv = p.count["v"]
        p.wait("a", "v", vv)
        aa = p.op("a", lambda e: e.activation(s_rst[0:1, 128:256], s_rst[0:1, 0:128], AF.Sqrt))
        p.wait("v", "a", aa)
        p.op("v", lambda e: e.reciprocal(s_rst[0:1, 0:128], s_rst[0:1, 128:256]))
        vv = p.op("v", lambda e: e.tensor_copy(s_rstb[0:1, 0:128], s_rst[0:1, 0:128]))
        p.wait("t", "v", vv)
        trt = p.op("t", lambda e: e.transpose(pRT[:, 0:1], s_rstb[0:1, 0:128], s_id[0:1, 0:1]))
        p.wait("v", "t", trt)
        p.op("v", lambda e: e.tensor_copy(s_rstc[:], pRT[:, 0:1]))
        for oh in range(2):
            ps = pC if oh == 0 else pD
            for kc in range(8):
                last = (kc == 7)
                bld = (lambda e, _k=kc, _oh=oh, _p=ps, _s=ogs: e.matmul(
                    _p[:], s_og[:, _s.start + _k * 128:_s.start + (_k + 1) * 128],
                    s_wo[:, _k * 1024 + _oh * 512:_k * 1024 + (_oh + 1) * 512],
                    start=(_k == 0), stop=(_k == 7)))
                tm = p.op("t", bld) if last else p.op0("t", bld)
            p.wait("v", "t", tm)
            p.op("v", lambda e, _oh=oh, _p=ps: e.tensor_scalar(
                s_s1[:, _oh * 512:(_oh + 1) * 512], _p[:], s_rstc[:, 0:1], None, ALU.mult))
        vdbg = p.count["v"]
        p.wait("g", "v", vdbg)
        p.wait("g", "a")
        p.dma("g", dbg_s1[tc * 128:(tc + 1) * 128, :], s_s1[:], sem="sdbg")
        p.dma("g", dbg_g1[tc * 128:(tc + 1) * 128, :], s_g1[:], sem="sdbg")
        p.wait("v", "g.sdbg")
        p.wait("a", "g.sdbg")
        p.wait("v", "s.lx%d" % par, l3)
        p.wait("v", "a")
        p.op("v", lambda e: e.tensor_tensor(s_s1[:], s_s1[:], s_g1[:], ALU.mult))
        vf = p.op("v", lambda e, _s=ogs: e.tensor_tensor(s_s1[:], s_s1[:], s_x[:, _s], ALU.add))
        p.wait("g", "v", vf)
        p.dma("g", out_o[tc * 128:(tc + 1) * 128, :], s_s1[:], sem="so")
        p.wait("v", "g.so")
        p.wait("t", "v")
    return p
